# version 13 (final)
# speedup vs baseline: 1.0293x; 1.0224x over previous
import math
import numpy as np
import ml_dtypes
import concourse.bass as bass
import concourse.mybir as mybir
from concourse.bass_utils import run_bass_kernel_spmd

F32 = mybir.dt.float32
BF16 = mybir.dt.bfloat16
I32 = mybir.dt.int32
AF = mybir.ActivationFunctionType
ALU = mybir.AluOpType
AX = mybir.AxisListType

NCORES = 8
D = 1024
NCH = 8
TPC = 2048
LLOCAL = 2048
BF = ml_dtypes.bfloat16


class Tile:
    __slots__ = ("h", "w", "readers", "name")

    def __init__(self, h, name):
        self.h = h
        self.w = None
        self.readers = []
        self.name = name

    def __getitem__(self, key):
        return self.h[key]

    @property
    def ap(self):
        return self.h[:]


class Prog:

    COMPUTE = ("pe", "act", "dve", "pool")
    QUEUES = ("qsp", "qact", "qpool", "qdve")

    def __init__(self, name="prog"):
        nc = bass.Bass("TRN2", target_bir_lowering=False)
        self.nc = nc
        self.name = name
        self._stack = []
        self.eng = {
            "pe": nc.tensor, "act": nc.scalar, "dve": nc.vector,
            "pool": nc.gpsimd,
            "qsp": nc.sync, "qact": nc.scalar, "qpool": nc.gpsimd,
            "qdve": nc.vector,
        }
        self.istream = {
            "pe": "pe", "act": "act", "dve": "dve", "pool": "pool",
            "qsp": "qsp", "qact": "act", "qpool": "pool", "qdve": "dve",
        }
        self.sem = {}
        self.cnt = {}
        self.seen = {}
        self.pending = {}
        self.dbg = False

    DMA_POOL = 8

    def __enter__(self):
        for s in self.COMPUTE:
            h = self.nc.semaphore(s)
            self.sem[s] = h.__enter__()
            self._stack.append(h)
            self.cnt[s] = 0
        self.qsems = {}
        self.qrr = {}
        self.qout = {}
        for q in self.QUEUES:
            self.qsems[q] = []
            self.qrr[q] = 0
            for i in range(self.DMA_POOL):
                key = f"{q}{i}"
                h = self.nc.semaphore(key)
                self.sem[key] = h.__enter__()
                self._stack.append(h)
                self.cnt[key] = 0
                self.qout[key] = False
                self.qsems[q].append(key)
                self.istream[key] = self.istream[q]
        for s in set(self.istream.values()):
            self.seen[s] = {}
        self._block = self.nc.Block()
        self._block.__enter__()
        return self

    def finish(self):
        sync = self.eng["qsp"]
        for s, c in self.cnt.items():
            if c:
                sync.wait_ge(self.sem[s], c)

    def __exit__(self, *exc):
        self._block.__exit__(*exc)
        while self._stack:
            self._stack.pop().__exit__(*exc)
        return False

    def sbuf(self, name, shape, dtype):
        name = getattr(self, "_tag", "") + name
        h = self.nc.sbuf_tensor(name, list(shape), dtype)
        t = Tile(h.__enter__(), name)
        t.readers = list(getattr(self, "_free_hazards", []))
        self._ctx = getattr(self, "_ctx", {})
        self._ctx[id(t)] = h
        self._stack.append(h)
        self._alloc_log = getattr(self, "_alloc_log", [])
        self._alloc_log.append(t)
        return t

    def stage(self, tag):
        import contextlib

        @contextlib.contextmanager
        def _cm():
            old_tag = getattr(self, "_tag", "")
            self._tag = tag + "_"
            self._alloc_log = getattr(self, "_alloc_log", [])
            mark = len(self._alloc_log)
            yield
            for t in reversed(self._alloc_log[mark:]):
                if id(t) in self._ctx:
                    self.free(t)
            del self._alloc_log[mark:]
            self._tag = old_tag
        return _cm()

    def free(self, t):
        h = self._ctx.pop(id(t))
        self._stack.remove(h)
        h.__exit__(None, None, None)
        hz = getattr(self, "_free_hazards_d", {})
        for m in ([t.w] if t.w is not None else []) + t.readers:
            if hz.get(m[0], 0) < m[1]:
                hz[m[0]] = m[1]
        self._free_hazards_d = hz
        self._free_hazards = [(s, c) for s, c in hz.items()]

    def psum(self, name, shape, dtype=F32):
        name = getattr(self, "_tag", "") + name
        h = self.nc.psum_tensor(name, list(shape), dtype)
        t = Tile(h.__enter__(), name)
        self._ctx = getattr(self, "_ctx", {})
        self._ctx[id(t)] = h
        self._stack.append(h)
        self._alloc_log = getattr(self, "_alloc_log", [])
        self._alloc_log.append(t)
        return t

    def dram_in(self, name, shape, dtype):
        return Tile(self.nc.dram_tensor(name, list(shape), dtype,
                                        kind="ExternalInput").ap(), name)

    def dram_out(self, name, shape, dtype):
        return Tile(self.nc.dram_tensor(name, list(shape), dtype,
                                        kind="ExternalOutput").ap(), name)

    def dram_tmp(self, name, shape, dtype, addr_space="Local"):
        name = getattr(self, "_tag", "") + name
        return Tile(self.nc.dram_tensor(name, list(shape), dtype,
                                        kind="Internal",
                                        addr_space=addr_space).ap(), name)

    def _wait(self, engname, marks):
        stream = self.istream[engname]
        seen = self.seen[stream]
        eng = self.eng[engname]
        for (s, c) in marks:
            if c and seen.get(s, 0) < c:
                eng.wait_ge(self.sem[s], c)
                seen[s] = c

    def emit(self, engname, make, reads=(), writes=(), dma=False, inc=True,
             war=()):
        stream = self.istream[engname]
        marks = []
        for t in reads:
            if t.w is not None:
                marks.append(t.w)
        for t in list(writes) + list(war):
            if t.w is not None:
                marks.append(t.w)
            marks.extend(t.readers)
        if engname == "pe":
            marks = [m for m in marks if m[0] != "pe"]
        if dma:
            q = engname
            key = self.qsems[q][self.qrr[q]]
            self.qrr[q] = (self.qrr[q] + 1) % self.DMA_POOL
            if self.qout[key]:
                self._wait_sem(engname, key, self.cnt[key])
            self.qout[key] = True
        else:
            key = engname
        self._wait(engname, marks)
        inst = make(self.eng[engname])
        step = 16 if dma else 1
        pend = self.pending.setdefault(stream, ([], []))
        if inc:
            inst.then_inc(self.sem[key], step)
            self.cnt[key] += step
            mark = (key, self.cnt[key])
            for t in pend[0] + list(reads):
                t.readers.append(mark)
            for t in pend[1] + list(writes):
                t.w = mark
                t.readers = []
            pend[0].clear()
            pend[1].clear()
        else:
            assert not dma
            pend[0].extend(reads)
            pend[1].extend(writes)
        return inst

    def _wait_sem(self, engname, key, val):
        stream = self.istream[engname]
        seen = self.seen[stream]
        if val and seen.get(key, 0) < val:
            self.eng[engname].wait_ge(self.sem[key], val)
            seen[key] = val

    def group(self, engname, makes, reads=(), writes=(), dma=False):
        for i, mk in enumerate(makes):
            last = i == len(makes) - 1
            self.emit(engname, mk, reads=reads, writes=writes if last else (),
                      war=() if last else writes, dma=dma, inc=last)

    def dma(self, q, out, in_, reads=(), writes=(), **kw):
        self.emit(q, lambda e: e.dma_start(out=out, in_=in_, **kw),
                  reads=reads, writes=writes, dma=True)



class Consts:
    def __init__(self, pr):
        self.ones_col_bf = pr.sbuf("c_ones_col_bf", [128, 1], BF16)
        self.ones_row_f = pr.sbuf("c_ones_row_f", [1, 128], F32)
        self.ones_row_bf = pr.sbuf("c_ones_row_bf", [1, 128], BF16)
        self.pool_f = pr.sbuf("c_pool_f", [128, 8], F32)
        pr.emit("dve", lambda v: v.memset(self.ones_col_bf.ap, 1.0),
                writes=[self.ones_col_bf])
        pr.emit("dve", lambda v: v.memset(self.ones_row_f.ap, 1.0),
                writes=[self.ones_row_f])
        pr.emit("dve", lambda v: v.memset(self.ones_row_bf.ap, 1.0),
                writes=[self.ones_row_bf])
        pr.emit("dve", lambda v: v.memset(self.pool_f[:, 0:1], 1e-5),
                writes=[self.pool_f])

    def const(self, pr, slot, value):
        pr.emit("dve", lambda v: v.memset(self.pool_f[:, slot:slot + 1], value),
                writes=[self.pool_f])
        return self.pool_f[:, slot:slot + 1]


def emit_ln_from_xb(pr, C, xb, xln, T=TPC, eps_slot=0, rows_name="lnrows",
                    ps_pool=None, gamma=None, beta=None, extra_scale=None,
                    gb_tile=None):
    nch = NCH
    rows = pr.sbuf(rows_name, [1, 3 * T], F32)
    sq = pr.sbuf(rows_name + "_sq", [128, 512], BF16)
    psa = ps_pool["row_a"]
    psb = ps_pool["row_b"]
    bcM = pr.sbuf(rows_name + "_bM", [128, T], BF16)
    bcR = pr.sbuf(rows_name + "_bR", [128, T], BF16)

    for s in range(0, T, 512):
        makes = []
        for c in range(nch):
            makes.append(
                lambda e, c=c, s=s: e.matmul(
                    psa[0:1, 0:512], C.ones_col_bf.ap,
                    xb[:, c * T + s: c * T + s + 512],
                    start=(c == 0), stop=(c == nch - 1)))
        pr.group("pe", makes, reads=[xb, C.ones_col_bf], writes=[psa])
        pr.emit("dve", lambda v, s=s: v.tensor_scalar(
            out=rows[0:1, s:s + 512], in0=psa[0:1, 0:512], scalar1=1.0 / D,
            scalar2=None, op0=ALU.mult), reads=[psa], writes=[rows])
        for c in range(nch):
            pr.emit("act", lambda e, c=c, s=s: e.activation(
                out=sq[:, 0:512], in_=xb[:, c * T + s: c * T + s + 512],
                func=AF.Square), reads=[xb], writes=[sq])
            sp = c == nch - 1
            pr.emit("pe", lambda e, c=c: e.matmul(
                psb[0:1, 0:512], C.ones_col_bf.ap, sq[:, 0:512],
                start=(c == 0), stop=(c == nch - 1)),
                reads=[sq, C.ones_col_bf],
                writes=[psb] if sp else [], war=() if sp else [psb], inc=True)
        pr.emit("dve", lambda v, s=s: v.tensor_scalar(
            out=rows[0:1, T + s: T + s + 512], in0=psb[0:1, 0:512],
            scalar1=1.0 / D, scalar2=None, op0=ALU.mult),
            reads=[psb], writes=[rows])
    pr.emit("dve", lambda v: v.tensor_tensor(
        out=rows[0:1, 2 * T:3 * T], in0=rows[0:1, 0:T], in1=rows[0:1, 0:T],
        op=ALU.mult), reads=[rows], writes=[rows])
    pr.emit("dve", lambda v: v.tensor_tensor(
        out=rows[0:1, T:2 * T], in0=rows[0:1, T:2 * T],
        in1=rows[0:1, 2 * T:3 * T], op=ALU.subtract),
        reads=[rows], writes=[rows])
    eps_ap = C.pool_f[0:1, eps_slot:eps_slot + 1]
    pr.emit("act", lambda e: e.activation(
        out=rows[0:1, T:2 * T], in_=rows[0:1, T:2 * T], func=AF.Ln,
        bias=eps_ap), reads=[rows, C.pool_f], writes=[rows])
    pr.emit("act", lambda e: e.activation(
        out=rows[0:1, T:2 * T], in_=rows[0:1, T:2 * T], func=AF.Exp,
        scale=-0.5), reads=[rows], writes=[rows])
    if extra_scale is not None:
        pr.emit("dve", lambda v: v.tensor_scalar(
            out=rows[0:1, T:2 * T], in0=rows[0:1, T:2 * T],
            scalar1=float(extra_scale), scalar2=None, op0=ALU.mult),
            reads=[rows], writes=[rows])
    pr.emit("dve", lambda v: v.tensor_tensor(
        out=rows[0:1, 2 * T:3 * T], in0=rows[0:1, 0:T],
        in1=rows[0:1, T:2 * T], op=ALU.mult), reads=[rows], writes=[rows])

    for s in range(0, T, 512):
        pr.emit("pe", lambda e, s=s: e.matmul(
            psa[0:128, 0:512], C.ones_row_f.ap,
            rows[0:1, T + s:T + s + 512], start=True, stop=True),
            reads=[rows, C.ones_row_f], writes=[psa])
        pr.emit("dve", lambda v, s=s: v.tensor_copy(
            out=bcR[:, s:s + 512], in_=psa[0:128, 0:512]),
            reads=[psa], writes=[bcR])
        pr.emit("pe", lambda e, s=s: e.matmul(
            psb[0:128, 0:512], C.ones_row_f.ap,
            rows[0:1, 2 * T + s:2 * T + s + 512], start=True, stop=True),
            reads=[rows, C.ones_row_f], writes=[psb])
        pr.emit("dve", lambda v, s=s: v.tensor_copy(
            out=bcM[:, s:s + 512], in_=psb[0:128, 0:512]),
            reads=[psb], writes=[bcM])
    for c in range(nch):
        eng = "dve" if c % 2 == 0 else "pool"
        pr.emit(eng, lambda v, c=c: v.tensor_tensor(
            out=xln[:, c * T:(c + 1) * T], in0=xb[:, c * T:(c + 1) * T],
            in1=bcR.ap, op=ALU.mult), reads=[xb, bcR], writes=[xln])
        pr.emit(eng, lambda v, c=c: v.tensor_tensor(
            out=xln[:, c * T:(c + 1) * T], in0=xln[:, c * T:(c + 1) * T],
            in1=bcM.ap, op=ALU.subtract), reads=[xln, bcM], writes=[xln])
        if gamma is not None:
            pr.emit(eng, lambda v, c=c: v.tensor_scalar(
                out=xln[:, c * T:(c + 1) * T], in0=xln[:, c * T:(c + 1) * T],
                scalar1=gamma[:, c:c + 1], scalar2=beta[:, c:c + 1],
                op0=ALU.mult, op1=ALU.add),
                reads=[xln] + ([gb_tile] if gb_tile is not None else []),
                writes=[xln])
    for t in (bcR, bcM, sq, rows):
        pr.free(t)



def dio(pr, io, name, shape, dtype, kind="in"):
    if io is not None and name in io:
        return io[name]
    if kind == "in":
        return pr.dram_in(name, shape, dtype)
    if kind == "out":
        return pr.dram_out(name, shape, dtype)
    return pr.dram_tmp(name, shape, dtype)



def build_f2(pr=None, C=None, io=None):
    T = TPC
    H = 3072
    HCH = H // 128
    standalone = pr is None
    if standalone:
        pr = Prog("f2")
        pr.__enter__()
    if True:
        x = dio(pr, io, "x", [NCH, 128, T], F32)
        wg = dio(pr, io, "wg_t", [NCH, 128, H], I32)
        wu = dio(pr, io, "wu_t", [NCH, 128, H], I32)
        wd = dio(pr, io, "wd_t", [HCH, 128, D], I32)
        xo = dio(pr, io, "xo", [NCH, 128, T], F32, kind="out")

        if C is None:
            C = Consts(pr)
        xb = pr.sbuf("xb", [128, NCH * T], BF16)
        xln = xb
        y = pr.sbuf("y", [128, HCH * T], BF16)
        wbufg = [pr.sbuf(f"wbufg{i}", [128, NCH * 128], BF16) for i in range(2)]
        wbufu = [pr.sbuf(f"wbufu{i}", [128, NCH * 128], BF16) for i in range(2)]
        wbufd = [pr.sbuf(f"wbufd{i}", [128, D], BF16) for i in range(2)]
        g_sb = pr.sbuf("g_sb", [128, T], BF16)
        u_sb = pr.sbuf("u_sb", [128, T], BF16)
        xres = pr.sbuf("xres", [128, 512], F32)
        ps_pool = {
            "row_a": pr.psum("ps_row_a", [128, 512]),
            "row_b": pr.psum("ps_row_b", [128, 512]),
        }
        ps_g = pr.psum("ps_g", [128, 512])
        ps_u = pr.psum("ps_u", [128, 512])
        ps_d = [pr.psum(f"ps_d{i}", [128, 512]) for i in range(2)]

        pr.dma("qpool", xb[:, :].rearrange("p (c t) -> p c t", c=NCH),
               x.h.rearrange("c p t -> p c t"), reads=[x], writes=[xb])
        emit_ln_from_xb(pr, C, xb, xln, ps_pool=ps_pool)

        for h in range(HCH):
            bg = wbufg[h % 2]
            bu = wbufu[h % 2]
            cols = slice(h * 128, (h + 1) * 128)
            pr.dma("qpool", bg[:, :].rearrange("p (c n) -> p c n", c=NCH),
                   wg.h[:, :, cols].rearrange("c p n -> p c n"),
                   reads=[wg], writes=[bg])
            pr.dma("qpool", bu[:, :].rearrange("p (c n) -> p c n", c=NCH),
                   wu.h[:, :, cols].rearrange("c p n -> p c n"),
                   reads=[wu], writes=[bu])
            for s in range(0, T, 512):
                mg = []
                mu = []
                for c in range(NCH):
                    st = c == 0
                    sp = c == NCH - 1
                    mg.append(lambda e, c=c, st=st, sp=sp, s=s, bg=bg: e.matmul(
                        ps_g[:, 0:512], bg[:, c * 128:(c + 1) * 128],
                        xln[:, c * T + s: c * T + s + 512], start=st, stop=sp))
                    mu.append(lambda e, c=c, st=st, sp=sp, s=s, bu=bu: e.matmul(
                        ps_u[:, 0:512], bu[:, c * 128:(c + 1) * 128],
                        xln[:, c * T + s: c * T + s + 512], start=st, stop=sp))
                pr.group("pe", mg, reads=[bg, xln], writes=[ps_g])
                pr.group("pe", mu, reads=[bu, xln], writes=[ps_u])
                pr.emit("act", lambda e, s=s: e.activation(
                    out=g_sb[:, s:s + 512], in_=ps_g[:, 0:512], func=AF.Silu,
                    scale=1.0 / 127.0), reads=[ps_g], writes=[g_sb])
                pr.emit("dve", lambda v, s=s: v.tensor_scalar(
                    out=u_sb[:, s:s + 512], in0=ps_u[:, 0:512],
                    scalar1=1.0 / 127.0, scalar2=None, op0=ALU.mult),
                    reads=[ps_u], writes=[u_sb])
                pr.emit("dve", lambda v, s=s, h=h: v.tensor_tensor(
                    out=y[:, h * T + s: h * T + s + 512], in0=g_sb[:, s:s + 512],
                    in1=u_sb[:, s:s + 512], op=ALU.mult),
                    reads=[g_sb, u_sb], writes=[y])

        ps_o = [ps_pool["row_a"], ps_pool["row_b"], ps_g, ps_u, ps_d[0], ps_d[1]]
        ps_e = pr.psum("ps_e", [128, 512])
        ps_f = pr.psum("ps_f", [128, 512])
        ps_o = ps_o + [ps_e, ps_f]
        for s in range(0, T, 512):
            for h in range(HCH):
                bd = wbufd[h % 2]
                pr.dma("qpool", bd.ap, wd.h[h], reads=[wd], writes=[bd])
                for o in range(NCH):
                    st = h == 0
                    sp = h == HCH - 1
                    pr.emit("pe", lambda e, h=h, o=o, s=s, st=st, sp=sp, bd=bd:
                            e.matmul(ps_o[o][:, 0:512],
                                     bd[:, o * 128:(o + 1) * 128],
                                     y[:, h * T + s: h * T + s + 512],
                                     start=st, stop=sp),
                            reads=[bd, y],
                            writes=[ps_o[o]] if sp else [],
                            war=[ps_o[o]] if st else (), inc=True)
            for o in range(NCH):
                pr.dma("qsp", xres.ap, x.h[o, :, s:s + 512],
                       reads=[x], writes=[xres])
                pr.emit("dve", lambda v, o=o, s=s: v.scalar_tensor_tensor(
                    out=xres.ap, in0=ps_o[o][:, 0:512], scalar=1.0 / 127.0,
                    in1=xres.ap, op0=ALU.mult, op1=ALU.add),
                    reads=[ps_o[o], xres], writes=[xres])
                pr.dma("qsp", xo.h[o, :, s:s + 512], xres.ap,
                       reads=[xres], writes=[xo])
    if standalone:
        pr.finish()
        pr.__exit__(None, None, None)
        return pr.nc



def emit_load_w(pr, name, w_dram, OC, q="qpool", nk=NCH):
    wbuf = pr.sbuf(name, [128, nk * OC], BF16)
    pr.dma(q, wbuf[:, :].rearrange("p (c n) -> p c n", c=nk),
           w_dram.h.rearrange("c p n -> p c n"), reads=[w_dram], writes=[wbuf])
    return wbuf


def emit_proj_fm(pr, xln, wbuf, out_sb, OC, T, ps, scale=None, bias=None,
                 out_off=0, nk=NCH, xln_off=0, bias_dep=None):
    for oc in range(OC // 128):
        for s in range(0, T, 512):
            makes = []
            for k in range(nk):
                makes.append(lambda e, k=k, oc=oc, s=s: e.matmul(
                    ps[:, 0:512], wbuf[:, k * OC + oc * 128: k * OC + (oc + 1) * 128],
                    xln[:, xln_off + k * T + s: xln_off + k * T + s + 512],
                    start=(k == 0), stop=(k == nk - 1)))
            pr.group("pe", makes, reads=[wbuf, xln], writes=[ps])
            dst = slice(out_off + oc * T + s, out_off + oc * T + s + 512)
            if bias is not None:
                pr.emit("dve", lambda v, dst=dst, oc=oc: v.tensor_scalar(
                    out=out_sb[:, dst], in0=ps[:, 0:512],
                    scalar1=bias[:, oc:oc + 1],
                    scalar2=None if scale is None else float(scale),
                    op0=ALU.add,
                    op1=ALU.bypass if scale is None else ALU.mult),
                    reads=[ps] + ([bias_dep] if bias_dep is not None else []),
                    writes=[out_sb])
            elif scale is not None:
                pr.emit("dve", lambda v, dst=dst: v.tensor_scalar(
                    out=out_sb[:, dst], in0=ps[:, 0:512], scalar1=float(scale),
                    scalar2=None, op0=ALU.mult), reads=[ps], writes=[out_sb])
            else:
                pr.emit("dve", lambda v, dst=dst: v.tensor_copy(
                    out=out_sb[:, dst], in_=ps[:, 0:512]),
                    reads=[ps], writes=[out_sb])



def build_f5_qkv(pr=None, C=None, io=None):
    T = TPC
    standalone = pr is None
    if standalone:
        pr = Prog("f5_qkv")
        pr.__enter__()
    if True:
        x = dio(pr, io, "x", [NCH, 128, T], F32)
        wq = dio(pr, io, "wq_t", [NCH, 128, D], F32)
        wk = dio(pr, io, "wk_t", [NCH, 128, D], F32)
        wv = dio(pr, io, "wv_t", [NCH, 128, D], F32)
        wqp = dio(pr, io, "wqp_t", [NCH, 128, D], F32)
        wkp = dio(pr, io, "wkp_t", [NCH, 128, D], F32)
        ctab_d = dio(pr, io, "ctab", [128, T], F32)
        stab_d = dio(pr, io, "stab", [128, T], F32)
        qf_d = dio(pr, io, "qf", [128, NCH, T], BF16, kind="out")
        kf_d = dio(pr, io, "kf", [128, NCH, T], BF16, kind="out")
        vt_d = dio(pr, io, "vt", [T // 128, 128, 1040], BF16, kind="out")

        if C is None:
            C = Consts(pr)
        xb = pr.sbuf("xb", [128, NCH * T], BF16)
        ps_pool = {"row_a": pr.psum("ps_row_a", [128, 512]),
                   "row_b": pr.psum("ps_row_b", [128, 512])}
        ps = pr.psum("ps_mm", [128, 512])
        ctab = pr.sbuf("ctab_s", [128, T], BF16)
        stab = pr.sbuf("stab_s", [128, T], BF16)
        q_sb = pr.sbuf("q_sb", [128, NCH * T], BF16)
        tmp_sb = pr.sbuf("tmp_sb", [128, NCH * T], BF16)
        k_sb = pr.sbuf("k_sb", [128, NCH * T], BF16)

        pr.dma("qpool", xb[:, :].rearrange("p (c t) -> p c t", c=NCH),
               x.h.rearrange("c p t -> p c t"), reads=[x], writes=[xb])
        pr.dma("qpool", ctab.ap, ctab_d.h, reads=[ctab_d], writes=[ctab])
        pr.dma("qpool", stab.ap, stab_d.h, reads=[stab_d], writes=[stab])
        emit_ln_from_xb(pr, C, xb, xb, ps_pool=ps_pool)
        xln = xb

        def rope(dst, prm):
            for c in range(NCH):
                eng = "dve" if c % 2 == 0 else "pool"
                sl = slice(c * T, (c + 1) * T)
                pr.emit(eng, lambda v, sl=sl, dst=dst: v.tensor_tensor(
                    out=dst[:, sl], in0=dst[:, sl], in1=ctab.ap, op=ALU.mult),
                    reads=[dst, ctab], writes=[dst])
                pr.emit(eng, lambda v, sl=sl, prm=prm: v.tensor_tensor(
                    out=prm[:, sl], in0=prm[:, sl], in1=stab.ap, op=ALU.mult),
                    reads=[prm, stab], writes=[prm])
                pr.emit(eng, lambda v, sl=sl, dst=dst, prm=prm: v.tensor_tensor(
                    out=dst[:, sl], in0=dst[:, sl], in1=prm[:, sl], op=ALU.add),
                    reads=[dst, prm], writes=[dst])

        for (wmain, wperm, dst, sc) in ((wq, wqp, q_sb, 0.125),
                                        (wk, wkp, k_sb, None)):
            wbuf = emit_load_w(pr, "w_" + wmain.name, wmain, D)
            emit_proj_fm(pr, xln, wbuf, dst, D, T, ps, scale=sc)
            pr.free(wbuf)
            wbuf = emit_load_w(pr, "w_" + wperm.name, wperm, D)
            emit_proj_fm(pr, xln, wbuf, tmp_sb, D, T, ps, scale=sc)
            pr.free(wbuf)
            rope(dst, tmp_sb)
        pr.dma("qsp", qf_d.h, q_sb[:, :].rearrange("p (c t) -> p c t", c=NCH),
               reads=[q_sb], writes=[qf_d])
        pr.dma("qsp", kf_d.h, k_sb[:, :].rearrange("p (c t) -> p c t", c=NCH),
               reads=[k_sb], writes=[kf_d])
        pr.free(k_sb)
        pr.free(tmp_sb)
        pr.free(q_sb)
        v_sb = pr.sbuf("v_sb", [128, (T // 128) * 1040], BF16)
        wbuf = emit_load_w(pr, "w_v", wv, D)
        pr.emit("dve", lambda v: v.memset(v_sb.ap, 1.0), writes=[v_sb])
        for tt in range(T // 128):
            for ds in range(2):
                makes = []
                for k in range(NCH):
                    makes.append(lambda e, k=k, tt=tt, ds=ds: e.matmul(
                        ps[:, 0:512],
                        xln[:, k * T + tt * 128: k * T + tt * 128 + 128],
                        wbuf[:, k * D + ds * 512: k * D + ds * 512 + 512],
                        start=(k == 0), stop=(k == NCH - 1)))
                pr.group("pe", makes, reads=[wbuf, xln], writes=[ps])
                pr.emit("dve", lambda v, tt=tt, ds=ds: v.tensor_copy(
                    out=v_sb[:, tt * 1040 + ds * 520: tt * 1040 + (ds + 1) * 520]
                        .rearrange("p (h n) -> p h n", h=8)[:, :, 0:64],
                    in_=ps[:, 0:512].rearrange("p (h n) -> p h n", h=8)),
                    reads=[ps], writes=[v_sb])
        pr.free(wbuf)

        pr.dma("qsp", vt_d.h.rearrange("t p n -> p t n"),
               v_sb[:, :].rearrange("p (t n) -> p t n", n=1040),
               reads=[v_sb], writes=[vt_d])
    if standalone:
        pr.finish()
        pr.__exit__(None, None, None)
        return pr.nc


def emit_attention(pr, C, cfg, kload, qload, vt, o_fm, bias_cols,
                   ps_S, ps_O, ps_R, mask_tiles=None, ldram=None):
    TQ, TK, NH = cfg["TQ"], cfg["TK"], cfg["NH"]
    region = cfg.get("region", lambda qb, kt: "own")
    maskf = cfg.get("mask", lambda qb, kt: None)
    farf = cfg.get("far", lambda qb, kt: None)
    nkt = TK // 128
    NSB = len(ps_S)
    tag = cfg["tag"]
    kbuf = [pr.sbuf(f"att_k{i}_{tag}", [128, TK], BF16) for i in range(2)]
    qbuf = [pr.sbuf(f"att_q{i}_{tag}", [128, 512], BF16) for i in range(2)]
    P_sb = [pr.sbuf(f"att_P{i}_{tag}", [128, 512], BF16) for i in range(NSB)]
    lr_tmp = pr.sbuf(f"att_lr_{tag}", [NH, TQ], F32)
    r_sb = pr.sbuf(f"att_r_{tag}", [128, 512], BF16)
    ltmp = pr.sbuf(f"att_lt_{tag}", [1, 512], F32)
    rtmp = pr.sbuf(f"att_rt_{tag}", [1, 512], F32)

    qi = 0
    for ch in range(NCH):
        kb = kbuf[ch % 2]
        kload(ch, kb)
        for qb in range(TQ // 512):
            kts = [kt for kt in range(nkt) if region(qb, kt) != "skip"]
            qt = qbuf[qi % 2]
            qi += 1
            qload(ch, qb, qt)
            tiles = [(j, kt, hi, half) for j, kt in enumerate(kts)
                     for hi, half in enumerate((0, 64))]
            nt = len(tiles)
            LA = 2

            def emit_S(t, kb=kb, qt=qt, qb=qb):
                j, kt, hi, half = tiles[t]
                sps = ps_S[t % NSB]
                fr = farf(qb, kt)
                pr.emit("pe", lambda e, kb=kb, qt=qt, sps=sps, kt=kt,
                        half=half, fr=fr: e.matmul(
                            sps[:, 0:512],
                            kb[half:half + 64, kt * 128:(kt + 1) * 128],
                            qt[half:half + 64, 0:512],
                            start=True, stop=(fr is None)),
                        reads=[kb, qt],
                        writes=[sps] if fr is None else [],
                        war=[sps] if fr is not None else (), inc=True)
                if fr is not None:
                    frow, fdep = fr
                    sl = slice(qb * 512, (qb + 1) * 512)
                    pr.emit("pe", lambda e, sps=sps, frow=frow, sl=sl:
                            e.matmul(sps[:, 0:512], C.ones_row_bf.ap,
                                     frow[0:1, sl], start=False,
                                     stop=True),
                            reads=[fdep, C.ones_row_bf], writes=[sps],
                            inc=True)

            def emit_rest(t, ch=ch, qb=qb):
                j, kt, hi, half = tiles[t]
                h = 2 * ch + hi
                reg = region(qb, kt)
                first = j == 0
                last = j == len(kts) - 1
                sps = ps_S[t % NSB]
                pb = P_sb[t % NSB]
                po = ps_O[hi]
                bc = 2 * h + (0 if reg == "own" else 1)
                pr.emit("act", lambda e, sps=sps, pb=pb, bc=bc:
                        e.activation(out=pb.ap, in_=sps[:, 0:512],
                                     func=AF.Exp,
                                     bias=bias_cols[:, bc:bc + 1]),
                        reads=[sps, bias_cols], writes=[pb])
                mt = maskf(qb, kt)
                if mt is not None:
                    if isinstance(mt, tuple):
                        mt, meng = mt
                    else:
                        meng = "pool" if (j + hi) % 2 == 0 else "dve"
                    pr.emit(meng, lambda v, pb=pb, mt=mt: v.tensor_tensor(
                        out=pb.ap, in0=pb.ap, in1=mt.ap, op=ALU.mult),
                        reads=[pb, mt], writes=[pb])
                pr.emit("pe", lambda e, kt=kt, h=h, first=first, last=last,
                        pb=pb, po=po: e.matmul(
                            po[0:65, 0:512],
                            vt[:, kt * 1040 + h * 65: kt * 1040 + (h + 1) * 65],
                            pb.ap, start=first, stop=last),
                        reads=[vt, pb], writes=[po] if last else (),
                        war=[po] if first else (), inc=True)

            for t in range(nt + LA):
                if t < nt:
                    emit_S(t)
                if t >= LA:
                    emit_rest(t - LA)
            for hi, half in enumerate((0, 64)):
                h = 2 * ch + hi
                po = ps_O[hi]
                pr.emit("dve", lambda v, half=half, ch=ch, qb=qb, po=po:
                        v.tensor_copy(
                            out=o_fm[half:half + 64,
                                     ch * TQ + qb * 512: ch * TQ + (qb + 1) * 512],
                            in_=po[0:64, 0:512]), reads=[po], writes=[o_fm])
                pr.emit("dve", lambda v, po=po: v.tensor_copy(
                    out=ltmp[0:1, 0:512], in_=po[64:65, 0:512]),
                    reads=[po], writes=[ltmp])
                pr.dma("qsp", ldram.h[h, qb * 512:(qb + 1) * 512]
                       .rearrange("x -> () x"), ltmp[0:1, 0:512],
                       reads=[ltmp], writes=[ldram])

    pr.dma("qsp", lr_tmp[0:NH, :], ldram.h, reads=[ldram], writes=[lr_tmp])
    pr.emit("act", lambda e: e.activation(out=lr_tmp.ap, in_=lr_tmp.ap,
                                          func=AF.Ln), reads=[lr_tmp],
            writes=[lr_tmp])
    pr.emit("act", lambda e: e.activation(out=lr_tmp.ap, in_=lr_tmp.ap,
                                          func=AF.Exp, scale=-1.0),
            reads=[lr_tmp], writes=[lr_tmp])
    pr.dma("qsp", ldram.h, lr_tmp[0:NH, :], reads=[lr_tmp], writes=[ldram])
    for c in range(NCH):
        for s in range(0, TQ, 512):
            for half in (0, 64):
                hh = 2 * c + (half // 64)
                pr.dma("qact", rtmp[0:1, 0:512],
                       ldram.h[hh, s:s + 512].rearrange("x -> () x"),
                       reads=[ldram], writes=[rtmp])
                pr.emit("pe", lambda e, s=s, half=half: e.matmul(
                    ps_R[half:half + 64, 0:512], C.ones_row_f[0:1, 0:64],
                    rtmp[0:1, 0:512],
                    start=True, stop=True), reads=[rtmp, C.ones_row_f],
                    writes=[ps_R] if half else [], war=[ps_R] if not half else ())
            pr.emit("dve", lambda v: v.tensor_copy(out=r_sb.ap,
                                                   in_=ps_R[:, 0:512]),
                    reads=[ps_R], writes=[r_sb])
            pr.emit("dve", lambda v, c=c, s=s: v.tensor_tensor(
                out=o_fm[:, c * TQ + s: c * TQ + s + 512],
                in0=o_fm[:, c * TQ + s: c * TQ + s + 512],
                in1=r_sb.ap, op=ALU.mult), reads=[o_fm, r_sb], writes=[o_fm])
    for b in reversed(kbuf + qbuf + P_sb + [lr_tmp, r_sb, ltmp, rtmp]):
        pr.free(b)


def build_f5_attn():
    return build_attn("full")


def build_f7_attn():
    return build_attn("causal")


def build_attn(variant, pr=None, C=None, io=None, gathered=False, f7cfg=None):
    TQ = TPC
    TK = 2 * TPC
    NH = 16
    standalone = pr is None
    if standalone:
        pr = Prog(f"attn_{variant}")
        pr.__enter__()
    if True:
        x = dio(pr, io, "x", [NCH, 128, TQ], F32)
        qf_d = dio(pr, io, "qf", [128, NCH, TQ], BF16)
        if gathered:
            kf_d = io["kf_g"]
            vt_d = io["vt_g"]
        else:
            kf_d = dio(pr, io, "kf", [128, NCH, TK], BF16)
            vt_d = dio(pr, io, "vt", [TK // 128, 128, 1040], BF16)
        wo_d = dio(pr, io, "wo_t", [NCH, 128, D], F32)
        bo_d = dio(pr, io, "b_out", [128, NCH], F32)
        ones_blk_d = dio(pr, io, "ones_blk_in", [128, NCH * 16], F32)
        if variant == "causal":
            mask4_d = dio(pr, io, "mask4", [8 if f7cfg else 4, 128, 512], F32)
            oflag_d = dio(pr, io, "oflag", [128, 1], F32)
        xo = dio(pr, io, "xo", [NCH, 128, TQ], F32, kind="out")
        ldram = pr.dram_tmp("ldram", [NH, TQ], F32)

        if C is None:
            C = Consts(pr)
        vt = pr.sbuf("vt_s", [128, (TK // 128) * 1040], BF16)
        o_fm = pr.sbuf("o_fm", [128, NCH * TQ], BF16)
        bias_cols = pr.sbuf("bias_cols", [128, 2 * NH], F32)
        ones_blk = pr.sbuf("ones_blk", [128, NCH * 16], BF16)
        bo = pr.sbuf("bo", [128, NCH], F32)
        nrm = pr.sbuf("nrm", [16, 32], F32)
        brow = pr.sbuf("brow", [1, 16], F32)
        sq_s = pr.sbuf("sq_s", [128, 512], BF16)
        xres = pr.sbuf("xres_f5", [128, 512], F32)
        nbuf = pr.sbuf("nbuf", [128, max(TQ, TK)], BF16)

        if variant == "causal":
            mask4 = [pr.sbuf(f"mask4_{i}", [128, 512], BF16) for i in range(4)]
            oflag = pr.sbuf("oflag_s", [128, 1], F32)
            for i in range(4):
                pr.dma("qpool", mask4[i].ap, mask4_d.h[i], reads=[mask4_d],
                       writes=[mask4[i]])
            pr.dma("qsp", oflag.ap, oflag_d.h, reads=[oflag_d], writes=[oflag])
        ps_S = [pr.psum(f"ps_S{i}", [128, 512]) for i in range(4)]
        ps_O = [pr.psum(f"ps_O{i}", [65, 512]) for i in range(2)]
        ps_R = pr.psum("ps_R", [128, 512])
        ps_n = pr.psum("ps_n", [16, 512])

        pr.dma("qsp", vt[:, :].rearrange("p (t n) -> p t n", n=1040),
               vt_d.h.rearrange("t p n -> p t n"), reads=[vt_d], writes=[vt])
        pr.dma("qpool", ones_blk.ap, ones_blk_d.h, reads=[ones_blk_d],
               writes=[ones_blk])
        pr.dma("qsp", bo.ap, bo_d.h, reads=[bo_d], writes=[bo])

        def maxnorm(src_d, TT, dstcol):
            pr.emit("dve", lambda v: v.memset(nrm[:, dstcol:dstcol + 1], 0.0),
                    writes=[nrm])
            for c in range(NCH):
                pr.dma("qsp", nbuf[:, 0:TT], src_d.h[:, c, :], reads=[src_d],
                       writes=[nbuf])
                for s in range(0, TT, 512):
                    pr.emit("act", lambda e, s=s: e.activation(
                        out=sq_s[:, 0:512], in_=nbuf[:, s:s + 512],
                        func=AF.Square), reads=[nbuf], writes=[sq_s])
                    pr.emit("pe", lambda e, c=c: e.matmul(
                        ps_n[0:16, 0:512], ones_blk[:, c * 16:(c + 1) * 16],
                        sq_s[:, 0:512], start=True, stop=True),
                        reads=[sq_s, ones_blk], writes=[ps_n])
                    pr.emit("dve", lambda v: v.tensor_reduce(
                        out=nrm[:, 31:32], in_=ps_n[0:16, 0:512], axis=AX.X,
                        op=ALU.max), reads=[ps_n], writes=[nrm])
                    pr.emit("dve", lambda v, dstcol=dstcol: v.tensor_tensor(
                        out=nrm[:, dstcol:dstcol + 1],
                        in0=nrm[:, dstcol:dstcol + 1],
                        in1=nrm[:, 31:32], op=ALU.max), reads=[nrm],
                        writes=[nrm])

        maxnorm(qf_d, TQ, 0)
        maxnorm(kf_d, TK, 1)
        pr.emit("act", lambda e: e.activation(out=nrm[:, 2:3], in_=nrm[:, 0:1],
                                              func=AF.Sqrt), reads=[nrm],
                writes=[nrm])
        pr.emit("act", lambda e: e.activation(out=nrm[:, 3:4], in_=nrm[:, 1:2],
                                              func=AF.Sqrt), reads=[nrm],
                writes=[nrm])
        pr.emit("dve", lambda v: v.tensor_tensor(
            out=nrm[:, 2:3], in0=nrm[:, 2:3], in1=nrm[:, 3:4], op=ALU.mult),
            reads=[nrm], writes=[nrm])
        pr.emit("dve", lambda v: v.tensor_scalar(
            out=nrm[:, 2:3], in0=nrm[:, 2:3], scalar1=-1.0, scalar2=None,
            op0=ALU.mult), reads=[nrm], writes=[nrm])
        pr.dma("qsp", ldram.h[0, 0:16].rearrange("x -> x ()"), nrm[0:16, 2:3],
               reads=[nrm], writes=[ldram])
        pr.dma("qsp", brow[0:1, 0:16], ldram.h[0, 0:16].rearrange("x -> () x"),
               reads=[ldram], writes=[brow])
        pr.emit("pe", lambda e: e.matmul(
            ps_R[:, 0:16], C.ones_row_f.ap, brow[0:1, 0:16],
            start=True, stop=True), reads=[brow, C.ones_row_f], writes=[ps_R])
        for two in range(2):
            pr.emit("dve", lambda v, two=two: v.tensor_copy(
                out=bias_cols[:, :].rearrange(
                    "p (h two) -> p h two", two=2)[:, :, two:two + 1],
                in_=ps_R[:, 0:16].rearrange("p h -> p h ()")),
                reads=[ps_R], writes=[bias_cols])
        if variant == "causal":
            pr.emit("dve", lambda v: v.tensor_tensor(
                out=bias_cols[:, :].rearrange(
                    "p (h two) -> p h two", two=2)[:, :, 1:2],
                in0=bias_cols[:, :].rearrange(
                    "p (h two) -> p h two", two=2)[:, :, 1:2],
                in1=oflag[:, 0:1].rearrange("p x -> p () x")
                    .broadcast_to([128, 16, 1]),
                op=ALU.add), reads=[bias_cols, oflag], writes=[bias_cols])

        def kload(ch, buf):
            pr.dma("qsp", buf.ap, kf_d.h[:, ch, :], reads=[kf_d], writes=[buf])

        def qload(ch, qb, buf):
            pr.dma("qact", buf.ap, qf_d.h[:, ch, qb * 512:(qb + 1) * 512],
                   reads=[qf_d], writes=[buf])

        cfg = dict(TQ=TQ, TK=TK, NH=NH, tag="att")
        if variant == "causal":
            def region(qb, kt):
                if kt >= 16:
                    return "other"
                if kt * 128 >= qb * 512 + 512:
                    return "skip"
                return "own"
            def maskfn(qb, kt):
                d = kt - 4 * qb
                if kt < 16 and 0 <= d < 4:
                    return mask4[d]
                return None
            cfg["region"] = region
            cfg["mask"] = maskfn
        emit_attention(pr, C, cfg, kload, qload, vt, o_fm, bias_cols,
                       ps_S, ps_O, ps_R, ldram=ldram)

        wo = emit_load_w(pr, "wo", wo_d, D)
        for oc in range(NCH):
            for s in range(0, TQ, 512):
                makes = []
                for k in range(NCH):
                    makes.append(lambda e, k=k, oc=oc, s=s: e.matmul(
                        ps_S[0][:, 0:512],
                        wo[:, k * D + oc * 128: k * D + (oc + 1) * 128],
                        o_fm[:, k * TQ + s: k * TQ + s + 512],
                        start=(k == 0), stop=(k == NCH - 1)))
                pr.group("pe", makes, reads=[wo, o_fm], writes=[ps_S[0]])
                pr.dma("qsp", xres.ap, x.h[oc, :, s:s + 512], reads=[x],
                       writes=[xres])
                pr.emit("dve", lambda v, oc=oc: v.scalar_tensor_tensor(
                    out=xres.ap, in0=ps_S[0][:, 0:512], scalar=bo[:, oc:oc + 1],
                    in1=xres.ap, op0=ALU.add, op1=ALU.add),
                    reads=[ps_S[0], xres, bo], writes=[xres])
                pr.dma("qsp", xo.h[oc, :, s:s + 512], xres.ap, reads=[xres],
                       writes=[xo])
        pr.finish()
    return pr.nc



def emit_attn_prologue(pr, C, tag, qf_d, kf_d, vt_d, ldram, scale_recip=8.0,
                       gathered=False):
    TQ, TK = TPC, 2 * TPC
    vt = pr.sbuf(f"vt_{tag}", [128, (TK // 128) * 1040], BF16)
    bias_cols = pr.sbuf(f"bias_cols_{tag}", [128, 32], F32)
    nrm = pr.sbuf(f"nrm_{tag}", [16, 32], F32)
    brow = pr.sbuf(f"brow_{tag}", [1, 16], F32)
    sq_s = pr.sbuf(f"sqs_{tag}", [128, 512], BF16)
    nbuf = pr.sbuf(f"nbuf_{tag}", [128, TK], BF16)
    ps_S = [pr.psum(f"psS{i}_{tag}", [128, 512]) for i in range(4)]
    ps_O = [pr.psum(f"psO{i}_{tag}", [65, 512]) for i in range(2)]
    ps_R = pr.psum(f"psR_{tag}", [128, 512])
    ps_n = pr.psum(f"psn_{tag}", [16, 512])

    if gathered:
        for r in range(2):
            pr.dma("qsp", vt[:, r * 16 * 1040:(r + 1) * 16 * 1040]
                   .rearrange("p (t n) -> p t n", n=1040),
                   vt_d.h[r].rearrange("t p n -> p t n"), reads=[vt_d],
                   writes=[vt])
    else:
        pr.dma("qsp", vt[:, :].rearrange("p (t n) -> p t n", n=1040),
               vt_d.h.rearrange("t p n -> p t n"), reads=[vt_d], writes=[vt])

    def maxnorm(src_d, TT, dstcol):
        pr.emit("dve", lambda v: v.memset(nrm[:, dstcol:dstcol + 1], 0.0),
                writes=[nrm])
        for c in range(NCH):
            if gathered and TT == TK:
                for r in range(2):
                    pr.dma("qsp", nbuf[:, r * TPC:(r + 1) * TPC],
                           src_d.h[r, :, c, :], reads=[src_d], writes=[nbuf])
            else:
                pr.dma("qsp", nbuf[:, 0:TT], src_d.h[:, c, :], reads=[src_d],
                       writes=[nbuf])
            for s in range(0, TT, 512):
                pr.emit("act", lambda e, s=s: e.activation(
                    out=sq_s[:, 0:512], in_=nbuf[:, s:s + 512],
                    func=AF.Square), reads=[nbuf], writes=[sq_s])
                pr.emit("pe", lambda e, c=c: e.matmul(
                    ps_n[0:16, 0:512], C.ones_blk[:, c * 16:(c + 1) * 16],
                    sq_s[:, 0:512], start=True, stop=True),
                    reads=[sq_s, C.ones_blk], writes=[ps_n])
                pr.emit("dve", lambda v: v.tensor_reduce(
                    out=nrm[:, 31:32], in_=ps_n[0:16, 0:512], axis=AX.X,
                    op=ALU.max), reads=[ps_n], writes=[nrm])
                pr.emit("dve", lambda v, dstcol=dstcol: v.tensor_tensor(
                    out=nrm[:, dstcol:dstcol + 1],
                    in0=nrm[:, dstcol:dstcol + 1],
                    in1=nrm[:, 31:32], op=ALU.max), reads=[nrm], writes=[nrm])

    maxnorm(qf_d, TQ, 0)
    maxnorm(kf_d, TK, 1)
    pr.emit("act", lambda e: e.activation(out=nrm[:, 2:3], in_=nrm[:, 0:1],
                                          func=AF.Sqrt), reads=[nrm],
            writes=[nrm])
    pr.emit("act", lambda e: e.activation(out=nrm[:, 3:4], in_=nrm[:, 1:2],
                                          func=AF.Sqrt), reads=[nrm],
            writes=[nrm])
    pr.emit("dve", lambda v: v.tensor_tensor(
        out=nrm[:, 2:3], in0=nrm[:, 2:3], in1=nrm[:, 3:4], op=ALU.mult),
        reads=[nrm], writes=[nrm])
    pr.emit("dve", lambda v: v.tensor_scalar(
        out=nrm[:, 2:3], in0=nrm[:, 2:3], scalar1=-1.0, scalar2=None,
        op0=ALU.mult), reads=[nrm], writes=[nrm])
    pr.dma("qsp", ldram.h[0, 0:16].rearrange("x -> x ()"), nrm[0:16, 2:3],
           reads=[nrm], writes=[ldram])
    pr.dma("qsp", brow[0:1, 0:16], ldram.h[0, 0:16].rearrange("x -> () x"),
           reads=[ldram], writes=[brow])
    pr.emit("pe", lambda e: e.matmul(
        ps_R[:, 0:16], C.ones_row_f.ap, brow[0:1, 0:16],
        start=True, stop=True), reads=[brow, C.ones_row_f], writes=[ps_R])
    for two in range(2):
        pr.emit("dve", lambda v, two=two: v.tensor_copy(
            out=bias_cols[:, :].rearrange(
                "p (h two) -> p h two", two=2)[:, :, two:two + 1],
            in_=ps_R[:, 0:16].rearrange("p h -> p h ()")),
            reads=[ps_R], writes=[bias_cols])
    for t in (nbuf, sq_s, brow, nrm):
        pr.free(t)
    return vt, bias_cols, dict(S=ps_S, O=ps_O, R=ps_R, n=ps_n)


def emit_attn_sublayer(pr, C, tag, qf_d, kf_d, vt_d, wo_d, bo, resid_in_d,
                       resid_out_d, cfgx, bo_dep=None, gathered=False):
    TQ, TK = TPC, 2 * TPC
    ldram = pr.dram_tmp(f"ldram_{tag}", [16, TQ], F32)
    vt, bias_cols, ps = emit_attn_prologue(pr, C, tag, qf_d, kf_d, vt_d, ldram,
                                           gathered=gathered)
    o_fm = pr.sbuf(f"ofm_{tag}", [128, NCH * TQ], BF16)
    xres = pr.sbuf(f"xres_{tag}", [128, 512], F32)

    if cfgx.get("oflag") is not None:
        oflag = cfgx["oflag"]
        pr.emit("dve", lambda v: v.tensor_tensor(
            out=bias_cols[:, :].rearrange(
                "p (h two) -> p h two", two=2)[:, :, 1:2],
            in0=bias_cols[:, :].rearrange(
                "p (h two) -> p h two", two=2)[:, :, 1:2],
            in1=oflag[:, 0:1].rearrange("p x -> p () x")
                .broadcast_to([128, 16, 1]),
            op=ALU.add), reads=[bias_cols, oflag], writes=[bias_cols])

    def kload(ch, buf):
        if gathered:
            for r in range(2):
                pr.dma("qsp", buf[:, r * TPC:(r + 1) * TPC],
                       kf_d.h[r, :, ch, :], reads=[kf_d], writes=[buf])
        else:
            pr.dma("qsp", buf.ap, kf_d.h[:, ch, :], reads=[kf_d], writes=[buf])

    def qload(ch, qb, buf):
        pr.dma("qact", buf.ap, qf_d.h[:, ch, qb * 512:(qb + 1) * 512],
               reads=[qf_d], writes=[buf])

    cfg = dict(TQ=TQ, TK=TK, NH=16, tag=tag)
    cfg.update({k: v for k, v in cfgx.items() if k in ("region", "mask")})
    emit_attention(pr, C, cfg, kload, qload, vt, o_fm, bias_cols,
                   ps["S"], ps["O"], ps["R"], ldram=ldram)

    wo = emit_load_w(pr, f"wo_{tag}", wo_d, D)
    for oc in range(NCH):
        for s in range(0, TQ, 512):
            makes = []
            for k in range(NCH):
                makes.append(lambda e, k=k, oc=oc, s=s: e.matmul(
                    ps["S"][0][:, 0:512],
                    wo[:, k * D + oc * 128: k * D + (oc + 1) * 128],
                    o_fm[:, k * TQ + s: k * TQ + s + 512],
                    start=(k == 0), stop=(k == NCH - 1)))
            pr.group("pe", makes, reads=[wo, o_fm], writes=[ps["S"][0]])
            pr.dma("qsp", xres.ap, resid_in_d.h[oc, :, s:s + 512],
                   reads=[resid_in_d], writes=[xres])
            pr.emit("dve", lambda v, oc=oc: v.scalar_tensor_tensor(
                out=xres.ap, in0=ps["S"][0][:, 0:512], scalar=bo[:, oc:oc + 1],
                in1=xres.ap, op0=ALU.add, op1=ALU.add),
                reads=[ps["S"][0], xres]
                + ([bo_dep] if bo_dep is not None else []), writes=[xres])
            pr.dma("qsp", resid_out_d.h[oc, :, s:s + 512], xres.ap,
                   reads=[xres], writes=[resid_out_d])
    pr.free(wo)
    pr.free(xres)
    pr.free(o_fm)
    for t in (ps["n"], ps["R"], ps["O"][1], ps["O"][0], ps["S"][3],
              ps["S"][2], ps["S"][1], ps["S"][0], bias_cols, vt):
        pr.free(t)


def emit_stream_cast(pr, x_d, xb, T=TPC):
    pr.dma("qpool", xb[:, :].rearrange("p (c t) -> p c t", c=NCH),
           x_d.h.rearrange("c p t -> p c t"), reads=[x_d], writes=[xb])


def emit_qkv_noperm(pr, C, tag, xln, wq_d, wk_d, wv_d, bq, bk, bv_row,
                    qf_d, kf_d, vt_d, ps, T=TPC, qscale=0.125, bias_dep=None):
    t_sb = pr.sbuf(f"qkv_t_{tag}", [128, NCH * T], BF16)
    for (wd, dstd, bias, sc) in ((wq_d, qf_d, bq, qscale), (wk_d, kf_d, bk, None)):
        wbuf = emit_load_w(pr, f"w_{tag}_{wd.name}", wd, D)
        emit_proj_fm(pr, xln, wbuf, t_sb, D, T, ps, scale=sc, bias=bias,
                     bias_dep=bias_dep)
        pr.free(wbuf)
        pr.dma("qsp", dstd.h, t_sb[:, :].rearrange("p (c t) -> p c t", c=NCH),
               reads=[t_sb], writes=[dstd])
    pr.free(t_sb)
    v_sb = pr.sbuf(f"qkv_v_{tag}", [128, (T // 128) * 1040], BF16)
    wbuf = emit_load_w(pr, f"w_{tag}_v", wv_d, D)
    pr.emit("dve", lambda v: v.memset(v_sb.ap, 1.0), writes=[v_sb])
    for tt in range(T // 128):
        for ds in range(2):
            makes = []
            for k in range(NCH):
                makes.append(lambda e, k=k, tt=tt, ds=ds: e.matmul(
                    ps[:, 0:512],
                    xln[:, k * T + tt * 128: k * T + tt * 128 + 128],
                    wbuf[:, k * D + ds * 512: k * D + ds * 512 + 512],
                    start=(k == 0), stop=False))
            makes.append(lambda e, tt=tt, ds=ds: e.matmul(
                ps[:, 0:512], C.ones_row_f.ap,
                bv_row[0:1, ds * 512:(ds + 1) * 512], start=False, stop=True))
            pr.group("pe", makes, reads=[wbuf, xln, bv_row, C.ones_row_f],
                     writes=[ps])
            pr.emit("dve", lambda v, tt=tt, ds=ds: v.tensor_copy(
                out=v_sb[:, tt * 1040 + ds * 520: tt * 1040 + (ds + 1) * 520]
                    .rearrange("p (h n) -> p h n", h=8)[:, :, 0:64],
                in_=ps[:, 0:512].rearrange("p (h n) -> p h n", h=8)),
                reads=[ps], writes=[v_sb])
    pr.free(wbuf)
    pr.dma("qsp", vt_d.h.rearrange("t p n -> p t n"),
           v_sb[:, :].rearrange("p (t n) -> p t n", n=1040),
           reads=[v_sb], writes=[vt_d])
    pr.free(v_sb)


def emit_mlp_4096(pr, C, tag, xln, w1_d, b1, w2_d, b2, resid_in_d,
                  resid_out_d, ps_pool, T=TPC, b2_dep=None):
    HB = 4096
    HCH = 32
    y = pr.sbuf(f"mlp_y_{tag}", [128, HCH * (T // 2)], BF16)
    w1buf = [pr.sbuf(f"mlp_w1{i}_{tag}", [128, NCH * 128], BF16)
             for i in range(2)]
    w2buf = [pr.sbuf(f"mlp_w2{i}_{tag}", [128, D], BF16) for i in range(2)]
    xres = pr.sbuf(f"mlp_xres_{tag}", [128, 512], F32)
    ps_h = ps_pool[0]
    ps_o = ps_pool
    TH = T // 2
    for th in range(2):
        toff = th * TH
        for hc in range(HCH):
            b1w = w1buf[hc % 2]
            cols = slice(hc * 128, (hc + 1) * 128)
            pr.dma("qpool", b1w[:, :].rearrange("p (c n) -> p c n", c=NCH),
                   w1_d.h[:, :, cols].rearrange("c p n -> p c n"),
                   reads=[w1_d], writes=[b1w])
            for s in range(0, TH, 512):
                makes = []
                for k in range(NCH):
                    makes.append(lambda e, k=k, s=s, b1w=b1w: e.matmul(
                        ps_h[:, 0:512], b1w[:, k * 128:(k + 1) * 128],
                        xln[:, k * T + toff + s: k * T + toff + s + 512],
                        start=(k == 0), stop=(k == NCH - 1)))
                pr.group("pe", makes, reads=[b1w, xln], writes=[ps_h])
                pr.emit("act", lambda e, hc=hc, s=s: e.activation(
                    out=y[:, hc * TH + s: hc * TH + s + 512], in_=ps_h[:, 0:512],
                    func=AF.Gelu, bias=b1[:, hc:hc + 1]),
                    reads=[ps_h, b1], writes=[y])
        for s in range(0, TH, 512):
            for hc in range(HCH):
                b2w = w2buf[hc % 2]
                pr.dma("qpool", b2w.ap, w2_d.h[hc], reads=[w2_d], writes=[b2w])
                for o in range(NCH):
                    st = hc == 0
                    sp = hc == HCH - 1
                    pr.emit("pe", lambda e, hc=hc, o=o, s=s, st=st, sp=sp,
                            b2w=b2w: e.matmul(
                                ps_o[o][:, 0:512],
                                b2w[:, o * 128:(o + 1) * 128],
                                y[:, hc * TH + s: hc * TH + s + 512],
                                start=st, stop=sp),
                            reads=[b2w, y], writes=[ps_o[o]] if sp else [],
                            war=[ps_o[o]] if st else (), inc=True)
            for o in range(NCH):
                pr.dma("qsp", xres.ap, resid_in_d.h[o, :, toff + s: toff + s + 512],
                       reads=[resid_in_d], writes=[xres])
                pr.emit("dve", lambda v, o=o: v.scalar_tensor_tensor(
                    out=xres.ap, in0=ps_o[o][:, 0:512], scalar=b2[:, o:o + 1],
                    in1=xres.ap, op0=ALU.add, op1=ALU.add),
                    reads=[ps_o[o], xres]
                    + ([b2_dep] if b2_dep is not None else []), writes=[xres])
                pr.dma("qsp", resid_out_d.h[o, :, toff + s: toff + s + 512],
                       xres.ap, reads=[xres], writes=[resid_out_d])
    for t in reversed([y] + w1buf + w2buf + [xres]):
        pr.free(t)


F3_NMASK = 24

def f3_maskfn_g3(mask_tiles):
    def maskfn(qb, kt):
        d = kt - 4 * qb
        if 0 <= d < 4:
            return mask_tiles[qb * 8 + d]
        d2 = kt - 16 - 4 * qb
        if 0 <= d2 < 4:
            return mask_tiles[qb * 8 + 4 + d2]
        if kt < 4 * qb:
            return None
        if d2 >= 4:
            return (mask_tiles[36 + qb], "dve")
        return (mask_tiles[32 + qb], "dve")
    return maskfn


def f3_maskfn_g(mask_tiles):
    def maskfn(qb, kt):
        if kt < 4 * qb:
            return None
        base = qb * 10
        d = kt - 4 * qb
        if d < 4:
            return mask_tiles[base + d]
        if kt < 16:
            return mask_tiles[base + 4]
        d2 = kt - 16 - 4 * qb
        if 0 <= d2 < 4:
            return mask_tiles[base + 5 + d2]
        if d2 >= 4:
            return mask_tiles[base + 9]
        return mask_tiles[base + 4]
    return maskfn


def f3_maskfn(mask_tiles):
    def maskfn(qb, kt):
        if kt >= 16:
            return mask_tiles[20 + qb]
        d = kt - 4 * qb
        if d < 0:
            return None
        if d < 4:
            return mask_tiles[qb * 4 + d]
        return mask_tiles[16 + qb]
    return maskfn


def build_f3_pre(pr=None, C=None, io=None):
    T = TPC
    standalone = pr is None
    if standalone:
        pr = Prog("f3_pre")
        pr.__enter__()
    if True:
        x = dio(pr, io, "x", [NCH, 128, T], F32)
        wq_d = dio(pr, io, "wq_t", [NCH, 128, D], F32)
        wk_d = dio(pr, io, "wk_t", [NCH, 128, D], F32)
        wv_d = dio(pr, io, "wv_t", [NCH, 128, D], F32)
        gb_d = dio(pr, io, "gb", [128, 6 * NCH], F32)
        bv_d = dio(pr, io, "bv_row", [1, D], F32)
        x3_d = dio(pr, io, "x3", [NCH, 128, T], F32, kind="out")
        qf_d = dio(pr, io, "qf", [128, NCH, T], BF16, kind="out")
        kf_d = dio(pr, io, "kf", [128, NCH, T], BF16, kind="out")
        vt_d = dio(pr, io, "vt", [T // 128, 128, 1040], BF16, kind="out")

        if C is None:
            C = Consts(pr)
        gb = pr.sbuf("gb_s", [128, 6 * NCH], F32)
        bv_row = pr.sbuf("bv_row_s", [1, D], F32)
        xb = pr.sbuf("xb", [128, NCH * T], BF16)
        x3f = pr.sbuf("x3f", [128, NCH * T], F32)
        ps_pool = {"row_a": pr.psum("ps_row_a", [128, 512]),
                   "row_b": pr.psum("ps_row_b", [128, 512])}
        ps = pr.psum("ps_mm", [128, 512])

        pr.dma("qsp", gb.ap, gb_d.h, reads=[gb_d], writes=[gb])
        pr.dma("qsp", bv_row.ap, bv_d.h, reads=[bv_d], writes=[bv_row])
        emit_stream_cast(pr, x, xb)
        emit_ln_from_xb(pr, C, xb, x3f, ps_pool=ps_pool, rows_name="ln_out")
        pr.dma("qsp", x3_d.h.rearrange("c p t -> p c t"),
               x3f[:, :].rearrange("p (c t) -> p c t", c=NCH),
               reads=[x3f], writes=[x3_d])
        for c in range(NCH):
            pr.emit("dve", lambda v, c=c: v.tensor_copy(
                out=xb[:, c * T:(c + 1) * T], in_=x3f[:, c * T:(c + 1) * T]),
                reads=[x3f], writes=[xb])
        pr.free(x3f)
        emit_ln_from_xb(pr, C, xb, xb, ps_pool=ps_pool, rows_name="ln1",
                        gamma=gb[:, 0:NCH], beta=gb[:, NCH:2 * NCH],
                        gb_tile=gb)
        emit_qkv_noperm(pr, C, "l0", xb, wq_d, wk_d, wv_d,
                        gb[:, 2 * NCH:3 * NCH], gb[:, 3 * NCH:4 * NCH],
                        bv_row, qf_d, kf_d, vt_d, ps, bias_dep=gb)
    if standalone:
        pr.finish()
        pr.__exit__(None, None, None)
        return pr.nc


def build_f3_mid(pr=None, C=None, io=None, gathered=False, gmask=False):
    T = TPC
    NM = 40 if gmask else F3_NMASK
    standalone = pr is None
    if standalone:
        pr = Prog("f3_mid")
        pr.__enter__()
    if True:
        x3_d = dio(pr, io, "x3", [NCH, 128, T], F32)
        qf_d = dio(pr, io, "qf", [128, NCH, T], BF16)
        if gathered:
            kf_d = io["kf_g"]
            vt_d = io["vt_g"]
        else:
            kf_d = dio(pr, io, "kf", [128, NCH, 2 * T], BF16)
            vt_d = dio(pr, io, "vt", [2 * T // 128, 128, 1040], BF16)
        masks_d = dio(pr, io, "masks", [NM, 128, 512], F32)
        wo_d = dio(pr, io, "wo_t", [NCH, 128, D], F32)
        w1_d = dio(pr, io, "w1_t", [NCH, 128, 4096], F32)
        w2_d = dio(pr, io, "w2_t", [32, 128, D], F32)
        wq_d = dio(pr, io, "wq_t", [NCH, 128, D], F32)
        wk_d = dio(pr, io, "wk_t", [NCH, 128, D], F32)
        wv_d = dio(pr, io, "wv_t", [NCH, 128, D], F32)
        gb_d = dio(pr, io, "gb", [128, 8 * NCH], F32)
        b1_d = dio(pr, io, "b1c", [128, 32], F32)
        bv_d = dio(pr, io, "bv_row", [1, D], F32)
        ob_d = dio(pr, io, "ones_blk_in", [128, NCH * 16], F32)
        x3o_d = dio(pr, io, "x3o", [NCH, 128, T], F32, kind="out")
        qf2_d = dio(pr, io, "qf2", [128, NCH, T], BF16, kind="out")
        kf2_d = dio(pr, io, "kf2", [128, NCH, T], BF16, kind="out")
        vt2_d = dio(pr, io, "vt2", [T // 128, 128, 1040], BF16, kind="out")

        if C is None:
            C = Consts(pr)
            C.ones_blk = pr.sbuf("ones_blk", [128, NCH * 16], BF16)
            pr.dma("qpool", C.ones_blk.ap, ob_d.h, reads=[ob_d],
                   writes=[C.ones_blk])
        gb = pr.sbuf("gb_s", [128, 8 * NCH], F32)
        b1 = pr.sbuf("b1_s", [128, 32], F32)
        bv_row = pr.sbuf("bv_row_s", [1, D], F32)
        pr.dma("qsp", gb.ap, gb_d.h, reads=[gb_d], writes=[gb])
        pr.dma("qsp", b1.ap, b1_d.h, reads=[b1_d], writes=[b1])
        pr.dma("qsp", bv_row.ap, bv_d.h, reads=[bv_d], writes=[bv_row])
        masks = [pr.sbuf(f"m{i}", [128, 512], BF16) for i in range(NM)]
        for i in range(NM):
            pr.dma("qpool", masks[i].ap, masks_d.h[i], reads=[masks_d],
                   writes=[masks[i]])

        if gmask:
            cfgx = dict(mask=f3_maskfn_g3(masks))
        else:
            cfgx = dict(mask=f3_maskfn(masks))
        emit_attn_sublayer(pr, C, "a0", qf_d, kf_d, vt_d, wo_d,
                           gb[:, 0:NCH], x3_d, x3o_d, cfgx, bo_dep=gb,
                           gathered=gathered)
        xb = pr.sbuf("xb", [128, NCH * T], BF16)
        ps_pool8 = [pr.psum(f"ps8_{i}", [128, 512]) for i in range(8)]
        pp = {"row_a": ps_pool8[0], "row_b": ps_pool8[1]}
        emit_stream_cast(pr, x3o_d, xb)
        emit_ln_from_xb(pr, C, xb, xb, ps_pool=pp, rows_name="ln2",
                        gamma=gb[:, NCH:2 * NCH], beta=gb[:, 2 * NCH:3 * NCH],
                        gb_tile=gb)
        emit_mlp_4096(pr, C, "m0", xb, w1_d, b1, w2_d, gb[:, 3 * NCH:4 * NCH],
                      x3o_d, x3o_d, ps_pool8, b2_dep=gb)
        emit_stream_cast(pr, x3o_d, xb)
        emit_ln_from_xb(pr, C, xb, xb, ps_pool=pp, rows_name="ln1b",
                        gamma=gb[:, 4 * NCH:5 * NCH], beta=gb[:, 5 * NCH:6 * NCH],
                        gb_tile=gb)
        emit_qkv_noperm(pr, C, "l1", xb, wq_d, wk_d, wv_d,
                        gb[:, 6 * NCH:7 * NCH], gb[:, 7 * NCH:8 * NCH],
                        bv_row, qf2_d, kf2_d, vt2_d, ps_pool8[0],
                        bias_dep=gb)
    if standalone:
        pr.finish()
        pr.__exit__(None, None, None)
        return pr.nc


def build_f3_post(pr=None, C=None, io=None, gathered=False, gmask=False):
    T = TPC
    NM = 40 if gmask else F3_NMASK
    standalone = pr is None
    if standalone:
        pr = Prog("f3_post")
        pr.__enter__()
    if True:
        x_d = dio(pr, io, "x", [NCH, 128, T], F32)
        x3_d = dio(pr, io, "x3", [NCH, 128, T], F32)
        qf_d = dio(pr, io, "qf", [128, NCH, T], BF16)
        if gathered:
            kf_d = io["kf_g"]
            vt_d = io["vt_g"]
        else:
            kf_d = dio(pr, io, "kf", [128, NCH, 2 * T], BF16)
            vt_d = dio(pr, io, "vt", [2 * T // 128, 128, 1040], BF16)
        masks_d = dio(pr, io, "masks", [NM, 128, 512], F32)
        wo_d = dio(pr, io, "wo_t", [NCH, 128, D], F32)
        w1_d = dio(pr, io, "w1_t", [NCH, 128, 4096], F32)
        w2_d = dio(pr, io, "w2_t", [32, 128, D], F32)
        gb_d = dio(pr, io, "gb", [128, 6 * NCH], F32)
        b1_d = dio(pr, io, "b1c", [128, 32], F32)
        ob_d = dio(pr, io, "ones_blk_in", [128, NCH * 16], F32)
        xo_d = dio(pr, io, "xo", [NCH, 128, T], F32, kind="out")
        x3t_d = pr.dram_tmp("x3t", [NCH, 128, T], F32)

        if C is None:
            C = Consts(pr)
            C.ones_blk = pr.sbuf("ones_blk", [128, NCH * 16], BF16)
            pr.dma("qpool", C.ones_blk.ap, ob_d.h, reads=[ob_d],
                   writes=[C.ones_blk])
        gb = pr.sbuf("gb_s", [128, 6 * NCH], F32)
        b1 = pr.sbuf("b1_s", [128, 32], F32)
        pr.dma("qsp", gb.ap, gb_d.h, reads=[gb_d], writes=[gb])
        pr.dma("qsp", b1.ap, b1_d.h, reads=[b1_d], writes=[b1])
        masks = [pr.sbuf(f"m{i}", [128, 512], BF16) for i in range(NM)]
        for i in range(NM):
            pr.dma("qpool", masks[i].ap, masks_d.h[i], reads=[masks_d],
                   writes=[masks[i]])

        if gmask:
            cfgx = dict(mask=f3_maskfn_g3(masks))
        else:
            cfgx = dict(mask=f3_maskfn(masks))
        emit_attn_sublayer(pr, C, "a1", qf_d, kf_d, vt_d, wo_d,
                           gb[:, 0:NCH], x3_d, x3t_d, cfgx, bo_dep=gb,
                           gathered=gathered)
        xb = pr.sbuf("xb", [128, NCH * T], BF16)
        ps_pool8 = [pr.psum(f"ps8_{i}", [128, 512]) for i in range(8)]
        pp = {"row_a": ps_pool8[0], "row_b": ps_pool8[1]}
        emit_stream_cast(pr, x3t_d, xb)
        emit_ln_from_xb(pr, C, xb, xb, ps_pool=pp, rows_name="ln2",
                        gamma=gb[:, NCH:2 * NCH], beta=gb[:, 2 * NCH:3 * NCH],
                        gb_tile=gb)
        emit_mlp_4096(pr, C, "m1", xb, w1_d, b1, w2_d, gb[:, 3 * NCH:4 * NCH],
                      x3t_d, x3t_d, ps_pool8, b2_dep=gb)
        x3f = pr.sbuf("x3f", [128, NCH * T], F32)
        emit_stream_cast(pr, x3t_d, xb)
        emit_ln_from_xb(pr, C, xb, x3f, ps_pool=pp, rows_name="lnf",
                        gamma=gb[:, 4 * NCH:5 * NCH], beta=gb[:, 5 * NCH:6 * NCH],
                        gb_tile=gb)
        xres = pr.sbuf("xres_p", [128, 512], F32)
        for oc in range(NCH):
            for s in range(0, T, 512):
                pr.dma("qsp", xres.ap, x_d.h[oc, :, s:s + 512], reads=[x_d],
                       writes=[xres])
                pr.emit("dve", lambda v, oc=oc, s=s: v.tensor_tensor(
                    out=xres.ap, in0=xres.ap,
                    in1=x3f[:, oc * T + s: oc * T + s + 512], op=ALU.add),
                    reads=[xres, x3f], writes=[xres])
                pr.dma("qsp", xo_d.h[oc, :, s:s + 512], xres.ap, reads=[xres],
                       writes=[xo_d])
    if standalone:
        pr.finish()
        pr.__exit__(None, None, None)
        return pr.nc



def build_f6_q(pr=None, C=None, io=None):
    T = TPC
    OC = 2048
    standalone = pr is None
    if standalone:
        pr = Prog("f6_q")
        pr.__enter__()
    if True:
        x = dio(pr, io, "x", [NCH, 128, T], F32)
        wq_d = dio(pr, io, "wq_t", [NCH, 128, OC], F32)
        bq_d = dio(pr, io, "bqc", [128, 16], F32)
        q_d = dio(pr, io, "q", [128, 16, T], BF16, kind="out")
        xln_d = dio(pr, io, "xln", [128, NCH, T], BF16, kind="out")
        st_d = dio(pr, io, "stats", [128, 4], F32, kind="out")

        if C is None:
            C = Consts(pr)
        bq = pr.sbuf("bq_s", [128, 16], F32)
        xb = pr.sbuf("xb", [128, NCH * T], BF16)
        q_sb = pr.sbuf("q_sb", [128, 16 * T], BF16)
        st = pr.sbuf("st_s", [128, 8], F32)
        sq = pr.sbuf("sq_q", [128, T], BF16)
        ps_pool = {"row_a": pr.psum("ps_row_a", [128, 512]),
                   "row_b": pr.psum("ps_row_b", [128, 512])}
        ps = pr.psum("ps_mm", [128, 512])

        pr.dma("qsp", bq.ap, bq_d.h, reads=[bq_d], writes=[bq])
        emit_stream_cast(pr, x, xb)
        emit_ln_from_xb(pr, C, xb, xb, ps_pool=ps_pool)
        pr.dma("qsp", xln_d.h, xb[:, :].rearrange("p (c t) -> p c t", c=NCH),
               reads=[xb], writes=[xln_d])
        wbuf = emit_load_w(pr, "wq", wq_d, OC)
        emit_proj_fm(pr, xb, wbuf, q_sb, OC, T, ps, bias=bq, bias_dep=bq)
        pr.free(wbuf)
        pr.dma("qsp", q_d.h, q_sb[:, :].rearrange("p (c t) -> p c t", c=16),
               reads=[q_sb], writes=[q_d])
        pr.emit("dve", lambda v: v.memset(st.ap, 0.0), writes=[st])
        for oc in range(16):
            col = oc % 2
            pr.emit("dve", lambda v, oc=oc: v.tensor_reduce(
                out=st[:, 4:5], in_=q_sb[:, oc * T:(oc + 1) * T], axis=AX.X,
                op=ALU.add), reads=[q_sb], writes=[st])
            pr.emit("dve", lambda v, col=col: v.tensor_tensor(
                out=st[:, col:col + 1], in0=st[:, col:col + 1],
                in1=st[:, 4:5], op=ALU.add), reads=[st], writes=[st])
            pr.emit("act", lambda e, oc=oc: e.activation(
                out=sq[:, 0:T], in_=q_sb[:, oc * T:(oc + 1) * T],
                func=AF.Square), reads=[q_sb], writes=[sq])
            pr.emit("dve", lambda v: v.tensor_reduce(
                out=st[:, 4:5], in_=sq[:, 0:T], axis=AX.X,
                op=ALU.add), reads=[sq], writes=[st])
            pr.emit("dve", lambda v, col=col: v.tensor_tensor(
                out=st[:, 2 + col:3 + col], in0=st[:, 2 + col:3 + col],
                in1=st[:, 4:5], op=ALU.add), reads=[st], writes=[st])
        pr.dma("qsp", st_d.h, st[:, 0:4], reads=[st], writes=[st_d])
    if standalone:
        pr.finish()
        pr.__exit__(None, None, None)
        return pr.nc


def build_f6_main(pr=None, C=None, io=None, stats_reduced=False):
    T = TPC
    NT = float(4 * 4096 * 8)
    standalone = pr is None
    if standalone:
        pr = Prog("f6_main")
        pr.__enter__()
    if True:
        x = dio(pr, io, "x", [NCH, 128, T], F32)
        xln_d = dio(pr, io, "xln", [128, NCH, T], BF16)
        q_d = dio(pr, io, "q", [128, 16, T], BF16)
        if stats_reduced:
            allst_d = io["stsum"]
        else:
            allst_d = dio(pr, io, "allst", [128, 32], F32)
        gbc_d = dio(pr, io, "gbc", [128, 4], F32)
        k0_d = dio(pr, io, "k0", [32, 128], F32)
        k1_d = dio(pr, io, "k1", [32, 128], F32)
        down_d = dio(pr, io, "down_t", [NCH, 128, 1024], F32)
        up_d = dio(pr, io, "up_t", [NCH, 128, 1024], F32)
        id_d = dio(pr, io, "ident", [128, 128], F32)
        xo = dio(pr, io, "xo", [NCH, 128, T], F32, kind="out")

        if C is None:
            C = Consts(pr)
        xln = pr.sbuf("xln_s", [128, NCH * T], BF16)
        q_sb = pr.sbuf("q_sb", [128, 16 * T], BF16)
        st = pr.sbuf("st", [128, 40], F32)
        gbc = pr.sbuf("gbc_s", [128, 4], F32)
        keys = pr.sbuf("keys", [32, 128 * 2], BF16)
        keyn = pr.sbuf("keyn", [32, 2], F32)
        k0n = pr.sbuf("k0n", [128, 64], BF16)
        ident = pr.sbuf("ident_s", [128, 128], BF16)
        C_f = pr.sbuf("C_f", [128, NCH * T], BF16)
        G = [pr.sbuf(f"G{i}", [128, 1024], F32) for i in range(2)]
        gtmp = [pr.sbuf(f"gtmp{i}", [128, 1024], F32) for i in range(2)]
        eh = [pr.sbuf(f"eh{i}", [128, 1024], BF16) for i in range(2)]
        AB = [pr.sbuf(f"AB{i}", [128, 512], F32) for i in range(2)]
        ABb = [pr.sbuf(f"ABb{i}", [128, 512], BF16) for i in range(2)]
        mx = [pr.sbuf(f"mx{i}", [128, 128], F32) for i in range(2)]
        rq = [pr.sbuf(f"rq{i}", [128, 16], F32) for i in range(2)]
        sA = [pr.sbuf(f"sA{i}", [128, 16], F32) for i in range(2)]
        sqs = [pr.sbuf(f"sqs{i}", [128, 128], BF16) for i in range(2)]
        xres = pr.sbuf("xres", [128, 512], F32)

        ps_s = [pr.psum(f"ps_s{i}", [128, 512]) for i in range(2)]
        ps_n = [pr.psum(f"ps_n{i}", [128, 16]) for i in range(2)]
        ps_e = [pr.psum(f"ps_e{i}", [128, 512]) for i in range(2)]
        ps_t = pr.psum("ps_t", [128, 128], BF16)
        ps_u = pr.psum("ps_u", [128, 512])

        pr.dma("qsp", xln[:, :].rearrange("p (c t) -> p c t", c=NCH), xln_d.h,
               reads=[xln_d], writes=[xln])
        pr.dma("qsp", q_sb[:, :].rearrange("p (c t) -> p c t", c=16), q_d.h,
               reads=[q_d], writes=[q_sb])
        if stats_reduced:
            pr.dma("qsp", st[:, 32:36], allst_d.h, reads=[allst_d], writes=[st])
        else:
            pr.dma("qsp", st[:, 0:32], allst_d.h, reads=[allst_d], writes=[st])
        pr.dma("qsp", gbc.ap, gbc_d.h, reads=[gbc_d], writes=[gbc])
        pr.dma("qpool", keys[:, 0:128], k0_d.h, reads=[k0_d], writes=[keys])
        pr.dma("qpool", keys[:, 128:256], k1_d.h, reads=[k1_d], writes=[keys])
        pr.dma("qpool", ident.ap, id_d.h, reads=[id_d], writes=[ident])

        if not stats_reduced:
            for j in range(4):
                pr.emit("dve", lambda v, j=j: v.tensor_copy(
                    out=st[:, 32 + j:33 + j], in_=st[:, j:j + 1]),
                    reads=[st], writes=[st])
                for i in range(1, 8):
                    pr.emit("dve", lambda v, i=i, j=j: v.tensor_tensor(
                        out=st[:, 32 + j:33 + j], in0=st[:, 32 + j:33 + j],
                        in1=st[:, 4 * i + j:4 * i + j + 1], op=ALU.add),
                        reads=[st], writes=[st])
        eps_col = C.const(pr, 1, 1e-5)
        for j in range(2):
            pr.emit("dve", lambda v, j=j: v.tensor_scalar(
                out=st[:, 36 + j:37 + j], in0=st[:, 32 + j:33 + j],
                scalar1=1.0 / NT, scalar2=None, op0=ALU.mult),
                reads=[st], writes=[st])
            pr.emit("dve", lambda v, j=j: v.tensor_scalar(
                out=st[:, 38 + j:39 + j], in0=st[:, 34 + j:35 + j],
                scalar1=1.0 / NT, scalar2=None, op0=ALU.mult),
                reads=[st], writes=[st])
            pr.emit("dve", lambda v, j=j: v.tensor_tensor(
                out=st[:, 32 + j:33 + j], in0=st[:, 36 + j:37 + j],
                in1=st[:, 36 + j:37 + j], op=ALU.mult),
                reads=[st], writes=[st])
            pr.emit("dve", lambda v, j=j: v.tensor_tensor(
                out=st[:, 38 + j:39 + j], in0=st[:, 38 + j:39 + j],
                in1=st[:, 32 + j:33 + j], op=ALU.subtract),
                reads=[st], writes=[st])
            pr.emit("act", lambda e, j=j: e.activation(
                out=st[:, 38 + j:39 + j], in_=st[:, 38 + j:39 + j],
                func=AF.Ln, bias=eps_col), reads=[st, C.pool_f],
                writes=[st])
            pr.emit("act", lambda e, j=j: e.activation(
                out=st[:, 38 + j:39 + j], in_=st[:, 38 + j:39 + j],
                func=AF.Exp, scale=-0.5), reads=[st], writes=[st])
            pr.emit("dve", lambda v, j=j: v.tensor_tensor(
                out=st[:, 32 + j:33 + j], in0=gbc[:, j:j + 1],
                in1=st[:, 38 + j:39 + j], op=ALU.mult),
                reads=[st, gbc], writes=[st])
            pr.emit("dve", lambda v, j=j: v.tensor_tensor(
                out=st[:, 36 + j:37 + j], in0=st[:, 36 + j:37 + j],
                in1=st[:, 32 + j:33 + j], op=ALU.mult),
                reads=[st], writes=[st])
            pr.emit("dve", lambda v, j=j: v.tensor_tensor(
                out=st[:, 34 + j:35 + j], in0=gbc[:, 2 + j:3 + j],
                in1=st[:, 36 + j:37 + j], op=ALU.subtract),
                reads=[st, gbc], writes=[st])
        for oc in range(16):
            j = oc % 2
            pr.emit("dve" if oc % 2 == 0 else "pool",
                    lambda v, oc=oc, j=j: v.tensor_scalar(
                        out=q_sb[:, oc * T:(oc + 1) * T],
                        in0=q_sb[:, oc * T:(oc + 1) * T],
                        scalar1=st[:, 32 + j:33 + j],
                        scalar2=st[:, 34 + j:35 + j],
                        op0=ALU.mult, op1=ALU.add),
                    reads=[q_sb, st], writes=[q_sb])

        for kk in range(2):
            sl = slice(kk * 128, (kk + 1) * 128)
            pr.emit("act", lambda e, sl=sl: e.activation(
                out=sqs[0][0:32, 0:128], in_=keys[0:32, sl], func=AF.Square),
                reads=[keys], writes=[sqs[0]])
            pr.emit("dve", lambda v, kk=kk: v.tensor_reduce(
                out=keyn[:, kk:kk + 1], in_=sqs[0][0:32, 0:128], axis=AX.X,
                op=ALU.add), reads=[sqs[0]], writes=[keyn])
            pr.emit("act", lambda e, kk=kk: e.activation(
                out=keyn[:, kk:kk + 1], in_=keyn[:, kk:kk + 1],
                func=AF.Sqrt), reads=[keyn], writes=[keyn])
            pr.emit("dve", lambda v, kk=kk: v.tensor_scalar_max(
                out=keyn[:, kk:kk + 1], in0=keyn[:, kk:kk + 1],
                scalar1=1e-12), reads=[keyn], writes=[keyn])
            pr.emit("dve", lambda v, kk=kk: v.reciprocal(
                out=keyn[:, kk:kk + 1], in_=keyn[:, kk:kk + 1]),
                reads=[keyn], writes=[keyn])
            pr.emit("dve", lambda v, sl=sl, kk=kk: v.tensor_scalar(
                out=keys[0:32, sl], in0=keys[0:32, sl],
                scalar1=keyn[:, kk:kk + 1], scalar2=None, op0=ALU.mult),
                reads=[keys, keyn], writes=[keys])
            pr.emit("pe", lambda e, sl=sl: e.transpose(
                out=ps_t[:, 0:32], in_=keys[0:32, sl],
                identity=ident[0:32, 0:32]), reads=[keys, ident],
                writes=[ps_t])
            pr.emit("dve", lambda v, kk=kk: v.tensor_copy(
                out=k0n[:, kk * 32:(kk + 1) * 32], in_=ps_t[:, 0:32]),
                reads=[ps_t], writes=[k0n])

        wdown = emit_load_w(pr, "wdown", down_d, 1024)

        def stage_A(tt):
            b = tt % 2
            toff = tt * 128
            for oc in range(16):
                kk = oc % 2
                pr.emit("pe", lambda e, oc=oc, kk=kk, toff=toff, b=b: e.matmul(
                    ps_s[b][:, oc * 32:(oc + 1) * 32],
                    q_sb[:, oc * T + toff: oc * T + toff + 128],
                    k0n[:, kk * 32:(kk + 1) * 32], start=True, stop=True),
                    reads=[q_sb, k0n], writes=[ps_s[b]] if oc == 15 else [],
                    war=[ps_s[b]] if oc == 0 else (), inc=True)
                pr.emit("act", lambda e, oc=oc, toff=toff, b=b: e.activation(
                    out=sqs[b][:, 0:128],
                    in_=q_sb[:, oc * T + toff: oc * T + toff + 128],
                    func=AF.Square), reads=[q_sb], writes=[sqs[b]])
                pr.emit("pe", lambda e, oc=oc, b=b: e.matmul(
                    ps_n[b][:, oc:oc + 1], sqs[b][:, 0:128],
                    C.ones_col_bf.ap,
                    start=True, stop=True), reads=[sqs[b], C.ones_col_bf],
                    writes=[ps_n[b]] if oc == 15 else [],
                    war=[ps_n[b]] if oc == 0 else (), inc=True)

        def stage_B(tt):
            b = tt % 2
            pr.emit("act", lambda e, b=b: e.activation(
                out=rq[b].ap, in_=ps_n[b][:, 0:16], func=AF.Sqrt),
                reads=[ps_n[b]], writes=[rq[b]])
            pr.emit("dve", lambda v, b=b: v.tensor_scalar_max(
                out=rq[b].ap, in0=rq[b].ap, scalar1=1e-12), reads=[rq[b]],
                writes=[rq[b]])
            pr.emit("dve", lambda v, b=b: v.reciprocal(
                out=rq[b].ap, in_=rq[b].ap), reads=[rq[b]], writes=[rq[b]])
            pr.emit("dve", lambda v, b=b: v.tensor_tensor(
                out=AB[b].ap.rearrange("p (g j) -> p g j", g=16),
                in0=ps_s[b][:, 0:512].rearrange("p (g j) -> p g j", g=16),
                in1=rq[b][:, :, None].broadcast_to([128, 16, 32]),
                op=ALU.mult), reads=[ps_s[b], rq[b]], writes=[AB[b]])
            for g in range(16):
                pr.emit("dve", lambda v, g=g, b=b: v.max(
                    out=mx[b][:, g * 8:(g + 1) * 8],
                    in_=AB[b][:, g * 32:(g + 1) * 32]), reads=[AB[b]],
                    writes=[mx[b]])
            pr.emit("act", lambda e, b=b: e.activation(
                out=ABb[b].ap, in_=AB[b].ap, func=AF.Exp), reads=[AB[b]],
                writes=[ABb[b]])
            pr.emit("dve", lambda v, b=b: v.tensor_tensor(
                out=AB[b].ap.rearrange("p (g j) -> p g j", g=16),
                in0=AB[b].ap.rearrange("p (g j) -> p g j", g=16),
                in1=mx[b][:, :].rearrange("p (g i) -> p g i", g=16)[:, :, 3:4]
                    .broadcast_to([128, 16, 32]),
                op=ALU.is_ge), reads=[AB[b], mx[b]], writes=[AB[b]])
            pr.emit("dve", lambda v, b=b: v.tensor_tensor(
                out=AB[b].ap, in0=AB[b].ap, in1=ABb[b].ap, op=ALU.mult),
                reads=[AB[b], ABb[b]], writes=[AB[b]])
            pr.emit("dve", lambda v, b=b: v.tensor_reduce(
                out=sA[b].ap, in_=AB[b].ap.rearrange("p (g j) -> p g j", g=16),
                axis=AX.X, op=ALU.add), reads=[AB[b]], writes=[sA[b]])
            pr.emit("dve", lambda v, b=b: v.reciprocal(
                out=sA[b].ap, in_=sA[b].ap), reads=[sA[b]], writes=[sA[b]])
            pr.emit("dve", lambda v, b=b: v.tensor_tensor(
                out=ABb[b].ap.rearrange("p (g j) -> p g j", g=16),
                in0=AB[b].ap.rearrange("p (g j) -> p g j", g=16),
                in1=sA[b][:, :, None].broadcast_to([128, 16, 32]),
                op=ALU.mult), reads=[AB[b], sA[b]], writes=[ABb[b]])
            for h in range(8):
                eng = "dve" if h % 2 == 0 else "pool"
                if h == 0:
                    pr.emit(eng, lambda v, h=h, b=b: v.tensor_tensor(
                        out=G[b].ap.rearrange("p (i j) -> p i j", i=32),
                        in0=ABb[b][:, 2 * h * 32:(2 * h + 1) * 32][:, :, None]
                            .broadcast_to([128, 32, 32]),
                        in1=ABb[b][:, (2 * h + 1) * 32:(2 * h + 2) * 32][:, None, :]
                            .broadcast_to([128, 32, 32]),
                        op=ALU.mult), reads=[ABb[b]], writes=[G[b]])
                else:
                    pr.emit(eng, lambda v, h=h, b=b: v.tensor_tensor(
                        out=gtmp[b].ap.rearrange("p (i j) -> p i j", i=32),
                        in0=ABb[b][:, 2 * h * 32:(2 * h + 1) * 32][:, :, None]
                            .broadcast_to([128, 32, 32]),
                        in1=ABb[b][:, (2 * h + 1) * 32:(2 * h + 2) * 32][:, None, :]
                            .broadcast_to([128, 32, 32]),
                        op=ALU.mult), reads=[ABb[b]], writes=[gtmp[b]])
                    pr.emit(eng, lambda v, b=b: v.tensor_tensor(
                        out=G[b].ap, in0=G[b].ap, in1=gtmp[b].ap, op=ALU.add),
                        reads=[G[b], gtmp[b]], writes=[G[b]])

        def stage_C(tt):
            b = tt % 2
            toff = tt * 128
            for es in range(2):
                makes = []
                for k in range(NCH):
                    makes.append(lambda e, k=k, es=es, toff=toff: e.matmul(
                        ps_e[es][:, 0:512],
                        xln[:, k * T + toff: k * T + toff + 128],
                        wdown[:, k * 1024 + es * 512: k * 1024 + es * 512 + 512],
                        start=(k == 0), stop=(k == NCH - 1)))
                pr.group("pe", makes, reads=[xln, wdown], writes=[ps_e[es]])
                pr.emit("act", lambda e, es=es, b=b: e.activation(
                    out=eh[b][:, es * 512:(es + 1) * 512],
                    in_=ps_e[es][:, 0:512],
                    func=AF.Gelu), reads=[ps_e[es]], writes=[eh[b]])

        def stage_D(tt):
            b = tt % 2
            toff = tt * 128
            pr.emit("dve", lambda v, b=b: v.tensor_tensor(
                out=eh[b].ap, in0=eh[b].ap, in1=G[b].ap, op=ALU.mult),
                reads=[eh[b], G[b]], writes=[eh[b]])
            for ec in range(NCH):
                pr.emit("pe", lambda e, ec=ec, b=b: e.transpose(
                    out=ps_t[:, 0:128], in_=eh[b][:, ec * 128:(ec + 1) * 128],
                    identity=ident.ap), reads=[eh[b], ident], writes=[ps_t])
                pr.emit("dve", lambda v, ec=ec, toff=toff: v.tensor_copy(
                    out=C_f[:, ec * T + toff: ec * T + toff + 128],
                    in_=ps_t[:, 0:128]), reads=[ps_t], writes=[C_f])

        for tt in range(16):
            stage_A(tt)
            stage_C(tt)
            if tt > 0:
                stage_D(tt - 1)
            stage_B(tt)
        stage_D(15)
        pr.free(wdown)

        wup = emit_load_w(pr, "wup", up_d, 1024)
        for oc in range(NCH):
            for s in range(0, T, 512):
                makes = []
                for k in range(NCH):
                    makes.append(lambda e, k=k, oc=oc, s=s: e.matmul(
                        ps_u[:, 0:512],
                        wup[:, k * 1024 + oc * 128: k * 1024 + (oc + 1) * 128],
                        C_f[:, k * T + s: k * T + s + 512],
                        start=(k == 0), stop=(k == NCH - 1)))
                pr.group("pe", makes, reads=[wup, C_f], writes=[ps_u])
                pr.dma("qsp", xres.ap, x.h[oc, :, s:s + 512], reads=[x],
                       writes=[xres])
                pr.emit("dve", lambda v: v.tensor_tensor(
                    out=xres.ap, in0=ps_u[:, 0:512], in1=xres.ap, op=ALU.add),
                    reads=[ps_u, xres], writes=[xres])
                pr.dma("qsp", xo.h[oc, :, s:s + 512], xres.ap, reads=[xres],
                       writes=[xo])
    if standalone:
        pr.finish()
        pr.__exit__(None, None, None)
        return pr.nc


def run_f6(x_parts, Wq, bq, bn_gamma, bn_beta, sub_key0, sub_key1,
           expert_down, expert_up):
    nc_a = get_nc("f6_q", build_f6_q)
    nc_b = get_nc("f6_main", build_f6_main)
    wq_t = chunk_major(Wq)
    bqc = np.ascontiguousarray(bq.reshape(16, 128).T.astype(np.float32))
    in_a = [{"x": x_parts[c], "wq_t": wq_t, "bqc": bqc}
            for c in range(NCORES)]
    ra = run_bass_kernel_spmd(nc_a, in_a, core_ids=list(range(NCORES)))
    allst = np.ascontiguousarray(np.concatenate(
        [ra.results[c]["stats"] for c in range(NCORES)], axis=1))
    gbc = np.ascontiguousarray(np.stack(
        [bn_gamma[0:128], bn_gamma[128:256],
         bn_beta[0:128], bn_beta[128:256]], axis=1).astype(np.float32))
    down_t = chunk_major(np.ascontiguousarray(expert_down.T))
    up_t = chunk_major(expert_up)
    ident = np.eye(128, dtype=np.float32)
    in_b = []
    for c in range(NCORES):
        in_b.append({"x": x_parts[c], "xln": ra.results[c]["xln"],
                     "q": ra.results[c]["q"], "allst": allst, "gbc": gbc,
                     "k0": sub_key0.astype(np.float32),
                     "k1": sub_key1.astype(np.float32),
                     "down_t": down_t, "up_t": up_t, "ident": ident})
    rb = run_bass_kernel_spmd(nc_b, in_b, core_ids=list(range(NCORES)))
    return [rb.results[c]["xo"] for c in range(NCORES)]



TEXT = TPC + 512
F1_DEBUG = False


def build_f1_pre(pr=None, C=None, io=None):
    T = TEXT
    TO = TPC
    standalone = pr is None
    if standalone:
        pr = Prog("f1_pre")
        pr.__enter__()
    if True:
        x = dio(pr, io, "x_ext", [NCH, 128, T], F32)
        wq_d = dio(pr, io, "wq_t", [NCH, 128, D], F32)
        wk_d = dio(pr, io, "wk_t", [NCH, 128, D], F32)
        wv_d = dio(pr, io, "wv_t", [NCH, 128, D], F32)
        wb_d = dio(pr, io, "wb_t", [NCH, 128, 4], F32)
        cw_d = dio(pr, io, "cw", [3, 128, NCH * 4], F32)
        fs_d = dio(pr, io, "fir_s", [128, NCH * 3], F32)
        fm_d = dio(pr, io, "fir_m", [128, NCH * 15], F32)
        fl_d = dio(pr, io, "fir_l", [128, NCH * 63], F32)
        gw_d = dio(pr, io, "gw_t", [NCH, 128, 20], F32)
        gbias_d = dio(pr, io, "gbias", [1, 20], F32)
        ltemp_d = dio(pr, io, "ltemp", [1, 4], F32)
        epsp_d = dio(pr, io, "epsp", [1, 4], F32)
        id_d = dio(pr, io, "ident", [128, 128], F32)
        qn_d = dio(pr, io, "qn", [128, NCH, TO], BF16, kind="out")
        kn_d = dio(pr, io, "kn", [128, NCH, TO], BF16, kind="out")
        vd_d = dio(pr, io, "vd", [128, NCH, TO], BF16, kind="out")
        bt_d = dio(pr, io, "beta_tm", [TO, 4], BF16, kind="out")
        pf_d = dio(pr, io, "probs", [24, TO], F32, kind="out")
        rest_d = dio(pr, io, "rest", [NCH, 128, TO], BF16, kind="out")

        if C is None:
            C = Consts(pr)
        eps_col6 = C.const(pr, 2, 1e-6)
        sh = io.get("shared") if io else None
        xb = pr.sbuf("xb", [128, NCH * T], BF16)
        cw = pr.sbuf("cw_s", [128, 3 * NCH * 4], F32)
        fw = sh["fw"] if sh else pr.sbuf("fw_s", [128, NCH * 81], F32)
        grow = pr.sbuf("grow", [1, 64], F32)
        ident = pr.sbuf("ident_s", [128, 128], BF16)
        ps_pool = {"row_a": pr.psum("ps_row_a", [128, 512]),
                   "row_b": pr.psum("ps_row_b", [128, 512])}
        ps = pr.psum("ps_mm", [128, 512])
        ps_r = sh["ps_r"] if sh else pr.psum("ps_r", [128, 512])
        ps_t = pr.psum("ps_tr", [128, 128], BF16)

        pr.dma("qsp", cw[:, :].rearrange("p (a b) -> p a b", a=3),
               cw_d.h.rearrange("a p b -> p a b"), reads=[cw_d], writes=[cw])
        pr.dma("qsp", fw[:, 0:NCH * 3], fs_d.h, reads=[fs_d], writes=[fw])
        pr.dma("qsp", fw[:, NCH * 3:NCH * 18], fm_d.h, reads=[fm_d], writes=[fw])
        pr.dma("qsp", fw[:, NCH * 18:NCH * 81], fl_d.h, reads=[fl_d], writes=[fw])
        pr.dma("qsp", grow[0:1, 0:20], gbias_d.h, reads=[gbias_d], writes=[grow])
        pr.dma("qsp", grow[0:1, 20:24], ltemp_d.h, reads=[ltemp_d], writes=[grow])
        pr.dma("qsp", grow[0:1, 24:28], epsp_d.h, reads=[epsp_d], writes=[grow])
        pr.dma("qpool", ident.ap, id_d.h, reads=[id_d], writes=[ident])

        emit_stream_cast(pr, x, xb)
        emit_ln_from_xb(pr, C, xb, xb, T=T, ps_pool=ps_pool)
        xln = xb

        br = pr.sbuf("beta_r", [4, TO], F32)
        btm = pr.sbuf("beta_tm_s", [128, 16 * 4], BF16)
        wbuf_b = emit_load_w(pr, "wb", wb_d, 4)
        for s in range(0, TO, 512):
            makes = []
            for k in range(NCH):
                makes.append(lambda e, k=k, s=s: e.matmul(
                    ps[0:4, 0:512], wbuf_b[:, k * 4:(k + 1) * 4],
                    xln[:, k * T + 512 + s: k * T + 512 + s + 512],
                    start=(k == 0), stop=(k == NCH - 1)))
            pr.group("pe", makes, reads=[wbuf_b, xln], writes=[ps])
            pr.emit("act", lambda e, s=s: e.activation(
                out=br[0:4, s:s + 512], in_=ps[0:4, 0:512], func=AF.Sigmoid),
                reads=[ps], writes=[br])
        pr.free(wbuf_b)
        pr.dma("qsp", pf_d.h[20:24, :], br[0:4, :], reads=[br], writes=[pf_d])
        brb = pr.sbuf("beta_rb", [4, TO], BF16)
        pr.emit("dve", lambda v: v.tensor_copy(out=brb[0:4, :], in_=br[0:4, :]),
                reads=[br], writes=[brb])
        for tt in range(16):
            pr.emit("pe", lambda e, tt=tt: e.transpose(
                out=ps_t[:, 0:4], in_=brb[0:4, tt * 128:(tt + 1) * 128],
                identity=ident[0:4, 0:4]), reads=[brb, ident], writes=[ps_t])
            pr.emit("dve", lambda v, tt=tt: v.tensor_copy(
                out=btm[:, tt * 4:(tt + 1) * 4], in_=ps_t[:, 0:4]),
                reads=[ps_t], writes=[btm])
        pr.dma("qsp", bt_d.h.rearrange("(tt p) h -> p tt h", p=128),
               btm[:, :].rearrange("p (tt h) -> p tt h", h=4),
               reads=[btm], writes=[bt_d])
        pr.free(brb)
        pr.free(btm)
        pr.free(br)

        gwbuf = emit_load_w(pr, "gw", gw_d, 20)
        probs_fm = pr.sbuf("probs_fm", [20, TO], F32)
        ptile = pr.sbuf("ptile", [128, 24], F32)
        ptb = pr.sbuf("ptb", [128, 20], BF16)
        trow = pr.sbuf("trow", [128, 12], F32)
        pr.emit("act", lambda e: e.activation(
            out=grow[0:1, 28:32], in_=grow[0:1, 20:24], func=AF.Exp,
            scale=-1.0), reads=[grow], writes=[grow])
        pr.emit("dve", lambda v: v.tensor_scalar_max(
            out=grow[0:1, 32:36], in0=grow[0:1, 24:28], scalar1=0.0),
            reads=[grow], writes=[grow])
        pr.emit("dve", lambda v: v.tensor_scalar_min(
            out=grow[0:1, 32:36], in0=grow[0:1, 32:36], scalar1=0.2),
            reads=[grow], writes=[grow])
        pr.emit("dve", lambda v: v.tensor_scalar(
            out=grow[0:1, 36:40], in0=grow[0:1, 32:36], scalar1=-5.0,
            scalar2=1.0, op0=ALU.mult, op1=ALU.add), reads=[grow],
            writes=[grow])
        for (src0, dst0) in ((28, 0), (36, 4), (32, 8)):
            pr.emit("pe", lambda e, src0=src0: e.matmul(
                ps_r[:, 0:4], C.ones_row_f.ap, grow[0:1, src0:src0 + 4],
                start=True, stop=True), reads=[grow, C.ones_row_f],
                writes=[ps_r])
            pr.emit("dve", lambda v, dst0=dst0: v.tensor_copy(
                out=trow[:, dst0:dst0 + 4], in_=ps_r[:, 0:4]), reads=[ps_r],
                writes=[trow])
        gv = lambda a: a.rearrange("p (h f) -> p h f", h=4)
        for tt in range(16):
            toff = 512 + tt * 128
            makes = []
            for k in range(NCH):
                makes.append(lambda e, k=k, toff=toff: e.matmul(
                    ps[:, 0:20], xln[:, k * T + toff: k * T + toff + 128],
                    gwbuf[:, k * 20:(k + 1) * 20],
                    start=(k == 0), stop=False))
            makes.append(lambda e: e.matmul(
                ps[:, 0:20], C.ones_row_f.ap, grow[0:1, 0:20],
                start=False, stop=True))
            pr.group("pe", makes, reads=[xln, gwbuf, grow, C.ones_row_f],
                     writes=[ps])
            pr.emit("dve", lambda v: v.tensor_reduce(
                out=ptile[:, 20:24], in_=gv(ps[:, 0:20]), axis=AX.X,
                op=ALU.max), reads=[ps], writes=[ptile])
            pr.emit("dve", lambda v: v.tensor_tensor(
                out=gv(ptile[:, 0:20]), in0=gv(ps[:, 0:20]),
                in1=ptile[:, 20:24][:, :, None].broadcast_to([128, 4, 5]),
                op=ALU.subtract), reads=[ps, ptile], writes=[ptile])
            pr.emit("dve", lambda v: v.tensor_tensor(
                out=gv(ptile[:, 0:20]), in0=gv(ptile[:, 0:20]),
                in1=trow[:, 0:4][:, :, None].broadcast_to([128, 4, 5]),
                op=ALU.mult), reads=[ptile, trow], writes=[ptile])
            pr.emit("act", lambda e: e.activation(
                out=ptile[:, 0:20], in_=ptile[:, 0:20], func=AF.Exp),
                reads=[ptile], writes=[ptile])
            pr.emit("dve", lambda v: v.tensor_reduce(
                out=ptile[:, 20:24], in_=gv(ptile[:, 0:20]), axis=AX.X,
                op=ALU.add), reads=[ptile], writes=[ptile])
            pr.emit("dve", lambda v: v.reciprocal(
                out=ptile[:, 20:24], in_=ptile[:, 20:24]), reads=[ptile],
                writes=[ptile])
            pr.emit("dve", lambda v: v.tensor_tensor(
                out=gv(ptile[:, 0:20]), in0=gv(ptile[:, 0:20]),
                in1=ptile[:, 20:24][:, :, None].broadcast_to([128, 4, 5]),
                op=ALU.mult), reads=[ptile], writes=[ptile])
            pr.emit("dve", lambda v: v.tensor_tensor(
                out=gv(ptile[:, 0:20]), in0=gv(ptile[:, 0:20]),
                in1=trow[:, 4:8][:, :, None].broadcast_to([128, 4, 5]),
                op=ALU.mult), reads=[ptile, trow], writes=[ptile])
            pr.emit("dve", lambda v: v.tensor_tensor(
                out=gv(ptile[:, 0:20]), in0=gv(ptile[:, 0:20]),
                in1=trow[:, 8:12][:, :, None].broadcast_to([128, 4, 5]),
                op=ALU.add), reads=[ptile, trow], writes=[ptile])
            pr.emit("dve", lambda v: v.tensor_copy(out=ptb.ap,
                                                   in_=ptile[:, 0:20]),
                    reads=[ptile], writes=[ptb])
            pr.emit("pe", lambda e: e.transpose(
                out=ps_t[0:20, 0:128], in_=ptb.ap, identity=ident.ap),
                reads=[ptb, ident], writes=[ps_t])
            pr.emit("dve", lambda v, tt=tt: v.tensor_copy(
                out=probs_fm[0:20, tt * 128:(tt + 1) * 128],
                in_=ps_t[0:20, 0:128]), reads=[ps_t], writes=[probs_fm])
        pr.free(trow)
        pr.free(ptb)
        pr.free(ptile)
        pr.dma("qsp", pf_d.h[0:20, :], probs_fm[0:20, :], reads=[probs_fm],
               writes=[pf_d])
        pr.free(probs_fm)
        pr.free(gwbuf)

        work2 = pr.sbuf("work2", [128, 512], BF16)
        rows2 = pr.sbuf("rows2", [1, TO], F32)
        cvtmp = pr.sbuf("cvtmp", [128, T - 448], F32)

        def conv4_silu(buf, cwi):
            n = T - 448
            for c in range(NCH):
                eng = "dve"
                base = c * T + 448
                for j in range(4):
                    sh = j - 3
                    if j == 0:
                        pr.emit(eng, lambda v, base=base, sh=sh, c=c, j=j, n=n:
                                v.tensor_scalar(
                                    out=cvtmp.ap,
                                    in0=buf[:, base + sh: base + sh + n],
                                    scalar1=cw[:, cwi * 32 + 4 * c + j:
                                               cwi * 32 + 4 * c + j + 1],
                                    scalar2=None, op0=ALU.mult),
                                reads=[buf, cw], writes=[cvtmp])
                    else:
                        pr.emit(eng, lambda v, base=base, sh=sh, c=c, j=j, n=n:
                                v.scalar_tensor_tensor(
                                    out=cvtmp.ap,
                                    in0=buf[:, base + sh: base + sh + n],
                                    scalar=cw[:, cwi * 32 + 4 * c + j:
                                              cwi * 32 + 4 * c + j + 1],
                                    in1=cvtmp.ap, op0=ALU.mult, op1=ALU.add),
                                reads=[buf, cw, cvtmp], writes=[cvtmp])
                pr.emit("act", lambda e, base=base, n=n: e.activation(
                    out=buf[:, base: base + n], in_=cvtmp.ap, func=AF.Silu),
                    reads=[cvtmp], writes=[buf])

        def l2norm_heads(src):
            for h in range(4):
                for s in range(0, TO, 512):
                    for ci, cc in enumerate((2 * h, 2 * h + 1)):
                        pr.emit("act", lambda e, cc=cc, s=s: e.activation(
                            out=work2[:, 0:512],
                            in_=src[:, cc * T + 512 + s: cc * T + 512 + s + 512],
                            func=AF.Square), reads=[src], writes=[work2])
                        pr.emit("pe", lambda e, ci=ci: e.matmul(
                            ps_r[0:1, 0:512], C.ones_col_bf.ap,
                            work2[:, 0:512], start=(ci == 0), stop=(ci == 1)),
                            reads=[work2, C.ones_col_bf],
                            writes=[ps_r] if ci == 1 else [],
                            war=[ps_r] if ci == 0 else (), inc=True)
                    pr.emit("dve", lambda v, s=s: v.tensor_copy(
                        out=rows2[0:1, s:s + 512], in_=ps_r[0:1, 0:512]),
                        reads=[ps_r], writes=[rows2])
                pr.emit("act", lambda e: e.activation(
                    out=rows2[0:1, :], in_=rows2[0:1, :], func=AF.Ln,
                    bias=eps_col6[0:1, :]), reads=[rows2, C.pool_f],
                    writes=[rows2])
                pr.emit("act", lambda e: e.activation(
                    out=rows2[0:1, :], in_=rows2[0:1, :], func=AF.Exp,
                    scale=-0.5), reads=[rows2], writes=[rows2])
                for s in range(0, TO, 512):
                    pr.emit("pe", lambda e, s=s: e.matmul(
                        ps_r[:, 0:512], C.ones_row_f.ap, rows2[0:1, s:s + 512],
                        start=True, stop=True), reads=[rows2, C.ones_row_f],
                        writes=[ps_r])
                    pr.emit("dve", lambda v: v.tensor_copy(
                        out=work2[:, 0:512], in_=ps_r[:, 0:512]),
                        reads=[ps_r], writes=[work2])
                    for cc in (2 * h, 2 * h + 1):
                        pr.emit("dve", lambda v, cc=cc, s=s: v.tensor_tensor(
                            out=src[:, cc * T + 512 + s: cc * T + 512 + s + 512],
                            in0=src[:, cc * T + 512 + s: cc * T + 512 + s + 512],
                            in1=work2[:, 0:512], op=ALU.mult),
                            reads=[src, work2], writes=[src])

        work = pr.sbuf("work", [128, NCH * T], BF16)
        for (wd, cwi, outd) in ((wq_d, 0, qn_d), (wk_d, 1, kn_d)):
            wbuf = emit_load_w(pr, "w_" + wd.name, wd, D)
            emit_proj_fm(pr, xln, wbuf, work, D, T, ps)
            pr.free(wbuf)
            conv4_silu(work, cwi)
            l2norm_heads(work)
            pr.dma("qsp", outd.h,
                   work[:, :].rearrange("p (c t) -> p c t", c=NCH)[:, :, 512:],
                   reads=[work], writes=[outd])
        pr.free(work)
        vdir = sh["vdir"] if sh else pr.sbuf("vdir", [128, NCH * T], BF16)
        wbuf = emit_load_w(pr, "w_" + wv_d.name, wv_d, D)
        emit_proj_fm(pr, xln, wbuf, vdir, D, T, ps)
        pr.free(wbuf)
        conv4_silu(vdir, 2)
        pr.dma("qsp", vd_d.h,
               vdir[:, :].rearrange("p (c t) -> p c t", c=NCH)[:, :, 512:],
               reads=[vdir], writes=[vd_d])

        rest = sh["rest_s"] if sh else pr.sbuf("rest_s", [128, NCH * TO], BF16)
        pbc = sh["pbc"] if sh else pr.sbuf("pbc", [128, TO], BF16)
        prow = sh["prow"] if sh else pr.sbuf("prow", [1, 512], F32)
        ftmp = sh["ftmp"] if sh else pr.sbuf("ftmp", [128, TO], F32)

        def bcast_prob(row_idx):
            for s in range(0, TO, 512):
                pr.dma("qact", prow[0:1, 0:512],
                       pf_d.h[row_idx, s:s + 512].rearrange("x -> () x"),
                       reads=[pf_d], writes=[prow])
                pr.emit("pe", lambda e, s=s: e.matmul(
                    ps_r[:, 0:512], C.ones_row_f.ap, prow[0:1, 0:512],
                    start=True, stop=True), reads=[prow, C.ones_row_f],
                    writes=[ps_r])
                pr.emit("dve", lambda v, s=s: v.tensor_copy(
                    out=pbc[:, s:s + 512], in_=ps_r[:, 0:512]),
                    reads=[ps_r], writes=[pbc])

        for h in range(4):
            bcast_prob(5 * h + 4)
            for ci, cc in enumerate((2 * h, 2 * h + 1)):
                eng = "dve" if ci == 0 else "pool"
                pr.emit(eng, lambda v, cc=cc: v.tensor_tensor(
                    out=rest[:, cc * TO:(cc + 1) * TO],
                    in0=vdir[:, cc * T + 512: cc * T + 512 + TO],
                    in1=pbc.ap, op=ALU.mult), reads=[vdir, pbc], writes=[rest])

        def fir_chunk(K, fo, cc):
            eng = "dve"
            base = cc * T + 512
            for j in range(K):
                sh = j - (K - 1)
                if j == 0:
                    pr.emit(eng, lambda v, base=base, sh=sh, j=j: v.tensor_scalar(
                        out=ftmp.ap, in0=vdir[:, base + sh: base + sh + TO],
                        scalar1=fw[:, fo + K * cc + j:fo + K * cc + j + 1],
                        scalar2=None, op0=ALU.mult),
                        reads=[vdir, fw], writes=[ftmp])
                else:
                    pr.emit(eng, lambda v, base=base, sh=sh, j=j:
                            v.scalar_tensor_tensor(
                                out=ftmp.ap,
                                in0=vdir[:, base + sh: base + sh + TO],
                                scalar=fw[:, fo + K * cc + j:fo + K * cc + j + 1],
                                in1=ftmp.ap, op0=ALU.mult, op1=ALU.add),
                            reads=[vdir, fw, ftmp], writes=[ftmp])
            pr.emit(eng, lambda v: v.tensor_tensor(
                out=ftmp.ap, in0=ftmp.ap, in1=pbc.ap, op=ALU.mult),
                reads=[ftmp, pbc], writes=[ftmp])
            pr.emit(eng, lambda v, cc=cc: v.tensor_tensor(
                out=rest[:, cc * TO:(cc + 1) * TO],
                in0=rest[:, cc * TO:(cc + 1) * TO], in1=ftmp.ap, op=ALU.add),
                reads=[rest, ftmp], writes=[rest])

        if sh is None:
            for (j, fo, K) in ((0, 0, 3), (1, NCH * 3, 15), (2, NCH * 18, 63)):
                for h in range(4):
                    bcast_prob(5 * h + j)
                    for cc in (2 * h, 2 * h + 1):
                        fir_chunk(K, fo, cc)
            pr.dma("qsp", rest_d.h.rearrange("c p t -> p c t"),
                   rest[:, :].rearrange("p (c t) -> p c t", c=NCH),
                   reads=[rest], writes=[rest_d])
    if standalone:
        pr.finish()
        pr.__exit__(None, None, None)
        return pr.nc


def fir_gen(pr, C, sh, pf_d, rest_d):
    T, TO = TEXT, TPC
    vdir, rest, pbc, prow, ftmp, fw, ps_r = (
        sh["vdir"], sh["rest_s"], sh["pbc"], sh["prow"], sh["ftmp"],
        sh["fw"], sh["ps_r"])
    for (j, fo, K) in ((0, 0, 3), (1, NCH * 3, 15), (2, NCH * 18, 63)):
        for h in range(4):
            for s in range(0, TO, 512):
                pr.dma("qact", prow[0:1, 0:512],
                       pf_d.h[5 * h + j, s:s + 512].rearrange("x -> () x"),
                       reads=[pf_d], writes=[prow])
                yield 1
                pr.emit("pe", lambda e, s=s: e.matmul(
                    ps_r[:, 0:512], C.ones_row_f.ap, prow[0:1, 0:512],
                    start=True, stop=True), reads=[prow, C.ones_row_f],
                    writes=[ps_r])
                yield 1
                pr.emit("dve", lambda v, s=s: v.tensor_copy(
                    out=pbc[:, s:s + 512], in_=ps_r[:, 0:512]),
                    reads=[ps_r], writes=[pbc])
                yield 1
            for cc in (2 * h, 2 * h + 1):
                base = cc * T + 512
                for jt in range(K):
                    shf = jt - (K - 1)
                    if jt == 0:
                        pr.emit("dve", lambda v, base=base, shf=shf, jt=jt,
                                cc=cc, fo=fo, K=K: v.tensor_scalar(
                                    out=ftmp.ap,
                                    in0=vdir[:, base + shf: base + shf + TO],
                                    scalar1=fw[:, fo + K * cc + jt:
                                               fo + K * cc + jt + 1],
                                    scalar2=None, op0=ALU.mult),
                                reads=[vdir, fw], writes=[ftmp])
                    else:
                        pr.emit("dve", lambda v, base=base, shf=shf, jt=jt,
                                cc=cc, fo=fo, K=K: v.scalar_tensor_tensor(
                                    out=ftmp.ap,
                                    in0=vdir[:, base + shf: base + shf + TO],
                                    scalar=fw[:, fo + K * cc + jt:
                                              fo + K * cc + jt + 1],
                                    in1=ftmp.ap, op0=ALU.mult, op1=ALU.add),
                                reads=[vdir, fw, ftmp], writes=[ftmp])
                    yield 1
                pr.emit("dve", lambda v: v.tensor_tensor(
                    out=ftmp.ap, in0=ftmp.ap, in1=pbc.ap, op=ALU.mult),
                    reads=[ftmp, pbc], writes=[ftmp])
                yield 1
                pr.emit("dve", lambda v, cc=cc: v.tensor_tensor(
                    out=rest[:, cc * TO:(cc + 1) * TO],
                    in0=rest[:, cc * TO:(cc + 1) * TO], in1=ftmp.ap,
                    op=ALU.add), reads=[rest, ftmp], writes=[rest])
                yield 1
    pr.dma("qsp", rest_d.h.rearrange("c p t -> p c t"),
           rest[:, :].rearrange("p (c t) -> p c t", c=NCH),
           reads=[rest], writes=[rest_d])
    yield 1


def build_f1_delta(pr=None, C=None, io=None, cc_mode=False, aux=None):
    L = 2 * TPC
    NG = L // 128
    standalone = pr is None
    if standalone:
        pr = Prog("f1_delta")
        pr.__enter__()
    if True:
        if cc_mode:
            cc1 = io["cc1_out"]
            cc2 = io["cc2_out"]
            cc3 = io["cc3_in"]
            selfl_d = io["selfl"]
            qn_d = kn_d = vd_d = be_d = dl_d = None
        else:
            qn_d = dio(pr, io, "qn", [2, 2, 128, L], BF16)
            kn_d = dio(pr, io, "kn", [2, 2, 128, L], BF16)
            vd_d = dio(pr, io, "vd", [2, 2, 128, L], BF16)
            be_d = dio(pr, io, "beta", [2, L], BF16)
            dl_d = dio(pr, io, "delta", [2, 2, 128, L], BF16, kind="out")
        m3_d = dio(pr, io, "m3", [3, 128, 128], F32)
        id_d = dio(pr, io, "ident", [128, 128], F32)
        if F1_DEBUG:
            dbgA = pr.dram_out("dbgA", [128, 128], BF16)
            dbgX = pr.dram_out("dbgX", [128, 512], BF16)
            dbgT = pr.dram_out("dbgT", [128, 128], BF16)
            dbgU = pr.dram_out("dbgU", [128, 256], BF16)

        C = Consts(pr)
        qnf = pr.sbuf("qnf", [128, 2 * L], BF16)
        knf = pr.sbuf("knf", [128, 2 * L], BF16)
        vdf = pr.sbuf("vdf", [128, 2 * L], BF16)
        dfm = pr.sbuf("dfm", [128, 2 * L], BF16)
        m3 = [pr.sbuf(f"m3_{i}", [128, 128], BF16) for i in range(3)]
        ident = pr.sbuf("ident_s", [128, 128], BF16)
        S_sb = pr.sbuf("S_sb", [128, 512], BF16)
        S_f32 = pr.sbuf("S_f32", [128, 512], F32)
        kntm = pr.sbuf("kntm", [128, 256], BF16)
        vtm = pr.sbuf("vtm", [128, 256], BF16)
        X = pr.sbuf("X", [128, 512], BF16)
        A_sb = pr.sbuf("A_sb", [128, 128], BF16)
        At_sb = pr.sbuf("At_sb", [128, 128], BF16)
        P2 = pr.sbuf("P2", [128, 128], BF16)
        P2t = pr.sbuf("P2t", [128, 128], BF16)
        tmpm = pr.sbuf("tmpm", [128, 128], BF16)
        attnT = pr.sbuf("attnT", [128, 128], BF16)
        wfm = pr.sbuf("wfm", [128, 256], BF16)
        ua = pr.sbuf("ua", [128, 256], BF16)
        o_sb = pr.sbuf("o_sb", [128, 256], BF16)
        bcol = pr.sbuf("bcol", [128, 1], F32)
        bcol2 = pr.sbuf("bcol2", [128, 2], F32)
        bcolb = pr.sbuf("bcolb", [128, 1], BF16)
        brow = pr.sbuf("brow", [1, 128], F32)
        brow_bc = pr.sbuf("brow_bc", [128, 128], BF16)

        ps_G = pr.psum("ps_G", [128, 128])
        ps_X = pr.psum("ps_X", [128, 512])
        ps_wS = pr.psum("ps_wS", [128, 256])
        ps_o = pr.psum("ps_o", [128, 256])
        S_ps2 = pr.psum("S_ps2", [128, 512])
        S_ps = [S_ps2, S_ps2]
        ps_tr = pr.psum("ps_tr", [128, 128], BF16)
        ps_br = pr.psum("ps_br", [128, 128])

        for i in range(3):
            pr.dma("qpool", m3[i].ap, m3_d.h[i], reads=[m3_d], writes=[m3[i]])
        pr.dma("qpool", ident.ap, id_d.h, reads=[id_d], writes=[ident])

        def transpose128(src_ap, src_dep, dst_ap, dst_dep):
            pr.emit("pe", lambda e: e.transpose(
                out=ps_tr[:, 0:128], in_=src_ap, identity=ident.ap),
                reads=[src_dep, ident], writes=[ps_tr])
            pr.emit("act", lambda s: s.copy(out=dst_ap,
                                            in_=ps_tr[:, 0:128]),
                    reads=[ps_tr], writes=[dst_dep])

        if cc_mode:
            selfl = pr.sbuf("selfl_s", [128, 2], F32)
            seltmp = [pr.sbuf(f"selt{i}", [128, TPC], BF16) for i in range(2)]
            betall = pr.sbuf("betall", [128, 128], BF16)
            pr.dma("qsp", selfl.ap, selfl_d.h, reads=[selfl_d],
                   writes=[selfl])
            pr.dma("qsp", betall[:, :].rearrange("p (g c) -> p g c", c=4),
                   cc2.h.rearrange("(g p) c -> p g c", p=128),
                   reads=[cc2], writes=[betall])

        def sel_load(sb, dst_sl, rowA, rowB):
            pr.dma("qsp", seltmp[0].ap, cc1.h[rowA:rowA + 128, :],
                   reads=[cc1], writes=[seltmp[0]])
            pr.dma("qact", seltmp[1].ap, cc1.h[rowB:rowB + 128, :],
                   reads=[cc1], writes=[seltmp[1]])
            pr.emit("dve", lambda v: v.tensor_scalar(
                out=seltmp[0].ap, in0=seltmp[0].ap, scalar1=selfl[:, 0:1],
                scalar2=None, op0=ALU.mult), reads=[seltmp[0], selfl],
                writes=[seltmp[0]])
            pr.emit("dve", lambda v: v.scalar_tensor_tensor(
                out=sb[:, dst_sl], in0=seltmp[1].ap, scalar=selfl[:, 1:2],
                in1=seltmp[0].ap, op0=ALU.mult, op1=ALU.add),
                reads=[seltmp[1], selfl, seltmp[0]], writes=[sb])

        for j in range(2):
            if cc_mode:
                for ti, sb in ((0, qnf), (1, knf), (2, vdf)):
                    for r in range(2):
                        for c in range(2):
                            base = r * 3072 + ti * 1024
                            rowA = base + (2 * j + c) * 128
                            rowB = base + (4 + 2 * j + c) * 128
                            sel_load(sb, slice(c * L + r * TPC,
                                               c * L + r * TPC + TPC),
                                     rowA, rowB)
            else:
                for (dd, sb) in ((qn_d, qnf), (kn_d, knf), (vd_d, vdf)):
                    pr.dma("qsp", sb[:, :].rearrange("p (c t) -> p c t", c=2),
                           dd.h[j].rearrange("c p t -> p c t"),
                           reads=[dd], writes=[sb])
            pr.emit("dve", lambda v: v.memset(S_sb.ap, 0.0), writes=[S_sb])
            pr.emit("dve", lambda v: v.memset(S_f32.ap, 0.0), writes=[S_f32])
            for g in range(NG):
                gsl = lambda c: slice(c * L + g * 128, c * L + (g + 1) * 128)
                for c in range(2):
                    transpose128(knf[:, gsl(c)], knf,
                                 kntm[:, c * 128:(c + 1) * 128], kntm)
                    transpose128(vdf[:, gsl(c)], vdf,
                                 vtm[:, c * 128:(c + 1) * 128], vtm)
                if cc_mode:
                    pr.emit("dve", lambda v, g=g, j=j: v.tensor_scalar(
                        out=bcol2[:, 0:1], in0=betall[:, g * 4 + j:g * 4 + j + 1],
                        scalar1=selfl[:, 0:1], scalar2=None, op0=ALU.mult),
                        reads=[betall, selfl], writes=[bcol2])
                    pr.emit("dve", lambda v, g=g, j=j: v.scalar_tensor_tensor(
                        out=bcol.ap,
                        in0=betall[:, g * 4 + 2 + j:g * 4 + 3 + j],
                        scalar=selfl[:, 1:2],
                        in1=bcol2[:, 0:1], op0=ALU.mult, op1=ALU.add),
                        reads=[betall, selfl, bcol2], writes=[bcol])
                    pr.emit("dve", lambda v: v.tensor_copy(
                        out=bcolb.ap, in_=bcol.ap), reads=[bcol],
                        writes=[bcolb])
                    pr.emit("pe", lambda e: e.transpose(
                        out=ps_tr[0:1, 0:128], in_=bcolb.ap,
                        identity=ident.ap), reads=[bcolb, ident],
                        writes=[ps_tr])
                    pr.emit("dve", lambda v: v.tensor_copy(
                        out=brow[0:1, :], in_=ps_tr[0:1, 0:128]),
                        reads=[ps_tr], writes=[brow])
                    del_unused = None
                else:
                    pr.dma("qpool", bcol.ap,
                           be_d.h[j, g * 128:(g + 1) * 128].rearrange("x -> x ()"),
                           reads=[be_d], writes=[bcol])
                    pr.dma("qpool", brow[0:1, :],
                           be_d.h[j, g * 128:(g + 1) * 128].rearrange("x -> () x"),
                           reads=[be_d], writes=[brow])
                pr.emit("pe", lambda e: e.matmul(
                    ps_br[:, 0:128], C.ones_row_f.ap, brow[0:1, :],
                    start=True, stop=True), reads=[brow, C.ones_row_f],
                    writes=[ps_br])
                pr.emit("act", lambda s: s.copy(
                    out=brow_bc.ap, in_=ps_br[:, 0:128]), reads=[ps_br],
                    writes=[brow_bc])
                pr.emit("dve", lambda v: v.tensor_scalar(
                    out=X[:, 0:256], in0=vtm.ap, scalar1=bcol[:, 0:1],
                    scalar2=None, op0=ALU.mult), reads=[vtm, bcol], writes=[X])
                pr.emit("dve", lambda v: v.tensor_scalar(
                    out=X[:, 256:512], in0=kntm.ap, scalar1=bcol[:, 0:1],
                    scalar2=None, op0=ALU.mult), reads=[kntm, bcol], writes=[X])
                for c in range(2):
                    pr.emit("pe", lambda e, c=c: e.matmul(
                        ps_G[:, 0:128], knf[:, gsl(c)], knf[:, gsl(c)],
                        start=(c == 0), stop=(c == 1)), reads=[knf],
                        writes=[ps_G] if c == 1 else [],
                        war=[ps_G] if c == 0 else (), inc=True)
                pr.emit("dve", lambda v: v.tensor_tensor(
                    out=tmpm.ap, in0=ps_G[:, 0:128], in1=m3[0].ap,
                    op=ALU.mult), reads=[ps_G, m3[0]], writes=[tmpm])
                pr.emit("dve", lambda v: v.tensor_scalar(
                    out=A_sb.ap, in0=tmpm.ap, scalar1=bcol[:, 0:1],
                    scalar2=None, op0=ALU.mult), reads=[tmpm, bcol],
                    writes=[A_sb])
                pr.emit("dve", lambda v: v.tensor_tensor(
                    out=tmpm.ap, in0=ps_G[:, 0:128], in1=m3[1].ap,
                    op=ALU.mult), reads=[ps_G, m3[1]], writes=[tmpm])
                pr.emit("dve", lambda v: v.tensor_tensor(
                    out=At_sb.ap, in0=tmpm.ap, in1=brow_bc.ap, op=ALU.mult),
                    reads=[tmpm, brow_bc], writes=[At_sb])
                for c in range(2):
                    pr.emit("pe", lambda e, c=c: e.matmul(
                        ps_G[:, 0:128], knf[:, gsl(c)], qnf[:, gsl(c)],
                        start=(c == 0), stop=(c == 1)), reads=[knf, qnf],
                        writes=[ps_G] if c == 1 else [],
                        war=[ps_G] if c == 0 else (), inc=True)
                pr.emit("dve", lambda v: v.tensor_tensor(
                    out=attnT.ap, in0=ps_G[:, 0:128], in1=m3[2].ap,
                    op=ALU.mult), reads=[ps_G, m3[2]], writes=[attnT])

                if F1_DEBUG and j == 0 and g == 0:
                    pr.dma("qsp", dbgA.h, A_sb.ap, reads=[A_sb], writes=[dbgA])
                    pr.dma("qsp", dbgT.h, At_sb.ap, reads=[At_sb],
                           writes=[dbgT])
                    pr.dma("qsp", dbgU.h, X[:, 0:256], reads=[X], writes=[dbgU])
                P, Pt = A_sb, At_sb
                Q, Qt = P2, P2t
                for k in range(7):
                    pr.emit("pe", lambda e: e.matmul(
                        ps_X[:, 0:512], ident.ap, X.ap, start=True,
                        stop=False), reads=[ident, X], war=[ps_X], inc=True)
                    pr.emit("pe", lambda e, Pt=Pt: e.matmul(
                        ps_X[:, 0:512], Pt.ap, X.ap, start=False, stop=True),
                        reads=[Pt, X], writes=[ps_X], inc=True)
                    pr.emit("act", lambda s: s.copy(
                        out=X.ap, in_=ps_X[:, 0:512]), reads=[ps_X],
                        writes=[X])
                    if F1_DEBUG and j == 0 and g == 0 and k == 0:
                        pr.dma("qsp", dbgX.h, X.ap, reads=[X], writes=[dbgX])
                    if k < 6:
                        pr.emit("pe", lambda e, P=P, Pt=Pt: e.matmul(
                            ps_G[:, 0:128], Pt.ap, P.ap, start=True,
                            stop=True), reads=[P, Pt], writes=[ps_G])
                        pr.emit("act", lambda s, Q=Q: s.copy(
                            out=Q.ap, in_=ps_G[:, 0:128]), reads=[ps_G],
                            writes=[Q])
                        pr.emit("pe", lambda e, P=P, Pt=Pt: e.matmul(
                            ps_G[:, 0:128], P.ap, Pt.ap, start=True,
                            stop=True), reads=[P, Pt], writes=[ps_G])
                        pr.emit("act", lambda s, Qt=Qt: s.copy(
                            out=Qt.ap, in_=ps_G[:, 0:128]), reads=[ps_G],
                            writes=[Qt])
                        P, Pt, Q, Qt = Q, Qt, P, Pt
                for c in range(2):
                    transpose128(X[:, 256 + c * 128: 256 + (c + 1) * 128], X,
                                 wfm[:, c * 128:(c + 1) * 128], wfm)
                for c in range(2):
                    pr.emit("pe", lambda e, c=c: e.matmul(
                        ps_wS[:, 0:256], wfm[:, c * 128:(c + 1) * 128],
                        S_sb[:, c * 256:(c + 1) * 256], start=(c == 0),
                        stop=(c == 1)), reads=[wfm, S_sb],
                        writes=[ps_wS] if c == 1 else [],
                        war=[ps_wS] if c == 0 else (), inc=True)
                pr.emit("dve", lambda v: v.tensor_tensor(
                    out=ua.ap, in0=X[:, 0:256], in1=ps_wS[:, 0:256],
                    op=ALU.subtract), reads=[X, ps_wS], writes=[ua])
                if F1_DEBUG and j == 0 and g == 0:
                    pr.dma("qsp", dbgU.h, ua.ap, reads=[ua], writes=[dbgU])
                for c in range(2):
                    pr.emit("pe", lambda e, c=c: e.matmul(
                        ps_o[:, 0:256], qnf[:, gsl(c)],
                        S_sb[:, c * 256:(c + 1) * 256], start=(c == 0),
                        stop=False), reads=[qnf, S_sb],
                        war=[ps_o] if c == 0 else (), inc=True)
                pr.emit("pe", lambda e: e.matmul(
                    ps_o[:, 0:256], attnT.ap, ua.ap, start=False, stop=True),
                    reads=[attnT, ua], writes=[ps_o], inc=True)
                pr.emit("act", lambda s: s.copy(
                    out=o_sb.ap, in_=ps_o[:, 0:256]), reads=[ps_o],
                    writes=[o_sb])
                for c in range(2):
                    transpose128(o_sb[:, c * 128:(c + 1) * 128], o_sb,
                                 dfm[:, gsl(c)], dfm)
                for c in range(2):
                    pr.emit("pe", lambda e, c=c: e.matmul(
                        S_ps2[:, c * 256:(c + 1) * 256],
                        kntm[:, c * 128:(c + 1) * 128], ua.ap,
                        start=True, stop=True),
                        reads=[kntm, ua],
                        writes=[S_ps2], inc=True)
                for c in range(2):
                    pr.emit("dve", lambda v, c=c: v.tensor_tensor(
                        out=S_f32[:, c * 256:(c + 1) * 256],
                        in0=S_f32[:, c * 256:(c + 1) * 256],
                        in1=S_ps2[:, c * 256:(c + 1) * 256], op=ALU.add),
                        reads=[S_f32, S_ps2], writes=[S_f32])
                    pr.emit("dve", lambda v, c=c: v.tensor_copy(
                        out=S_sb[:, c * 256:(c + 1) * 256],
                        in_=S_f32[:, c * 256:(c + 1) * 256]),
                        reads=[S_f32], writes=[S_sb])
                if aux is not None:
                    for _ in range(13):
                        if next(aux, None) is None:
                            aux = None
                            break
            if cc_mode:
                for c in range(2):
                    pr.dma("qsp",
                           cc3.h[(2 * j + c) * 128:(2 * j + c + 1) * 128, :],
                           dfm[:, c * L:(c + 1) * L], reads=[dfm],
                           writes=[cc3])
            else:
                pr.dma("qsp", dl_d.h[j].rearrange("c p t -> p c t"),
                       dfm[:, :].rearrange("p (c t) -> p c t", c=2),
                       reads=[dfm], writes=[dl_d])
        if aux is not None:
            for _ in aux:
                pass
    if standalone:
        pr.finish()
        pr.__exit__(None, None, None)
        return pr.nc


def build_f1_post(pr=None, C=None, io=None, cc_mode=False):
    T = TPC
    standalone = pr is None
    if standalone:
        pr = Prog("f1_post")
        pr.__enter__()
    if True:
        x = dio(pr, io, "x", [NCH, 128, T], F32)
        rest_d = dio(pr, io, "rest", [NCH, 128, T], BF16)
        if cc_mode:
            dl_d = io["cc3_out"]
            selfl_d = io["selfl"]
        else:
            dl_d = dio(pr, io, "delta", [128, NCH, T], BF16)
        pf_d = dio(pr, io, "probs", [24, T], F32)
        nw_d = dio(pr, io, "normw", [128, 2], F32)
        wo_d = dio(pr, io, "wo_t", [NCH, 128, D], F32)
        xo = dio(pr, io, "xo", [NCH, 128, T], F32, kind="out")

        if C is None:
            C = Consts(pr)
        eps_col = C.const(pr, 3, 1e-5)
        o_fm = pr.sbuf("o_fm", [128, NCH * T], BF16)
        dl = pr.sbuf("dl", [128, NCH * T], BF16)
        nw = pr.sbuf("nw", [128, 2], F32)
        pbc = pr.sbuf("pbc", [128, T], BF16)
        prow = pr.sbuf("prow", [1, T], F32)
        rows = pr.sbuf("rows", [1, T], F32)
        work2 = pr.sbuf("work2", [128, 512], BF16)
        xres = pr.sbuf("xres", [128, 512], F32)
        ps = pr.psum("ps_mm", [128, 512])
        ps_r = pr.psum("ps_r", [128, 512])

        pr.dma("qsp", nw.ap, nw_d.h, reads=[nw_d], writes=[nw])
        if cc_mode:
            selfl = pr.sbuf("selfl_s", [128, 2], F32)
            pr.dma("qsp", selfl.ap, selfl_d.h, reads=[selfl_d],
                   writes=[selfl])
            seltmp = [pr.sbuf(f"selt{i}", [128, T], BF16) for i in range(2)]
            for gc in range(8):
                r, jc = gc // 4, gc % 4
                row0 = (r * 4 + jc) * 128
                pr.dma("qsp", seltmp[0].ap, dl_d.h[row0:row0 + 128, 0:T],
                       reads=[dl_d], writes=[seltmp[0]])
                pr.dma("qact", seltmp[1].ap, dl_d.h[row0:row0 + 128, T:2 * T],
                       reads=[dl_d], writes=[seltmp[1]])
                pr.emit("dve", lambda v: v.tensor_scalar(
                    out=seltmp[0].ap, in0=seltmp[0].ap, scalar1=selfl[:, 0:1],
                    scalar2=None, op0=ALU.mult), reads=[seltmp[0], selfl],
                    writes=[seltmp[0]])
                pr.emit("dve", lambda v, gc=gc: v.scalar_tensor_tensor(
                    out=dl[:, gc * T:(gc + 1) * T], in0=seltmp[1].ap,
                    scalar=selfl[:, 1:2], in1=seltmp[0].ap, op0=ALU.mult,
                    op1=ALU.add), reads=[seltmp[1], selfl, seltmp[0]],
                    writes=[dl])
        else:
            pr.dma("qsp", dl[:, :].rearrange("p (c t) -> p c t", c=NCH), dl_d.h,
                   reads=[dl_d], writes=[dl])
        pr.dma("qpool", o_fm[:, :].rearrange("p (c t) -> p c t", c=NCH),
               rest_d.h.rearrange("c p t -> p c t"), reads=[rest_d],
               writes=[o_fm])

        for h in range(4):
            pr.dma("qact", prow[0:1, :],
                   pf_d.h[5 * h + 3, :].rearrange("x -> () x"),
                   reads=[pf_d], writes=[prow])
            for s in range(0, T, 512):
                pr.emit("pe", lambda e, s=s: e.matmul(
                    ps_r[:, 0:512], C.ones_row_f.ap, prow[0:1, s:s + 512],
                    start=True, stop=True), reads=[prow, C.ones_row_f],
                    writes=[ps_r])
                pr.emit("dve", lambda v, s=s: v.tensor_copy(
                    out=pbc[:, s:s + 512], in_=ps_r[:, 0:512]),
                    reads=[ps_r], writes=[pbc])
            for ci, cc in enumerate((2 * h, 2 * h + 1)):
                eng = "dve" if ci == 0 else "pool"
                pr.emit(eng, lambda v, cc=cc: v.tensor_tensor(
                    out=dl[:, cc * T:(cc + 1) * T],
                    in0=dl[:, cc * T:(cc + 1) * T], in1=pbc.ap, op=ALU.mult),
                    reads=[dl, pbc], writes=[dl])
                pr.emit(eng, lambda v, cc=cc: v.tensor_tensor(
                    out=o_fm[:, cc * T:(cc + 1) * T],
                    in0=o_fm[:, cc * T:(cc + 1) * T],
                    in1=dl[:, cc * T:(cc + 1) * T], op=ALU.add),
                    reads=[o_fm, dl], writes=[o_fm])
            for s in range(0, T, 512):
                for ci, cc in enumerate((2 * h, 2 * h + 1)):
                    pr.emit("act", lambda e, cc=cc, s=s: e.activation(
                        out=work2[:, 0:512],
                        in_=o_fm[:, cc * T + s: cc * T + s + 512],
                        func=AF.Square), reads=[o_fm], writes=[work2])
                    pr.emit("pe", lambda e, ci=ci: e.matmul(
                        ps_r[0:1, 0:512], C.ones_col_bf.ap, work2[:, 0:512],
                        start=(ci == 0), stop=(ci == 1)),
                        reads=[work2, C.ones_col_bf],
                        writes=[ps_r] if ci == 1 else [],
                        war=[ps_r] if ci == 0 else (), inc=True)
                pr.emit("dve", lambda v, s=s: v.tensor_copy(
                    out=rows[0:1, s:s + 512], in_=ps_r[0:1, 0:512]),
                    reads=[ps_r], writes=[rows])
            pr.emit("act", lambda e: e.activation(
                out=rows[0:1, :], in_=rows[0:1, :], func=AF.Ln,
                scale=1.0 / 256.0, bias=eps_col[0:1, :]),
                reads=[rows, C.pool_f], writes=[rows])
            pr.emit("act", lambda e: e.activation(
                out=rows[0:1, :], in_=rows[0:1, :], func=AF.Exp, scale=-0.5),
                reads=[rows], writes=[rows])
            for s in range(0, T, 512):
                pr.emit("pe", lambda e, s=s: e.matmul(
                    ps_r[:, 0:512], C.ones_row_f.ap, rows[0:1, s:s + 512],
                    start=True, stop=True), reads=[rows, C.ones_row_f],
                    writes=[ps_r])
                pr.emit("dve", lambda v: v.tensor_copy(
                    out=work2[:, 0:512], in_=ps_r[:, 0:512]), reads=[ps_r],
                    writes=[work2])
                for ci, cc in enumerate((2 * h, 2 * h + 1)):
                    pr.emit("dve", lambda v, cc=cc, s=s, ci=ci: v.tensor_tensor(
                        out=o_fm[:, cc * T + s: cc * T + s + 512],
                        in0=o_fm[:, cc * T + s: cc * T + s + 512],
                        in1=work2[:, 0:512], op=ALU.mult),
                        reads=[o_fm, work2], writes=[o_fm])
                    pr.emit("dve", lambda v, cc=cc, s=s, ci=ci: v.tensor_scalar(
                        out=o_fm[:, cc * T + s: cc * T + s + 512],
                        in0=o_fm[:, cc * T + s: cc * T + s + 512],
                        scalar1=nw[:, ci:ci + 1], scalar2=None, op0=ALU.mult),
                        reads=[o_fm, nw], writes=[o_fm])
        wo = emit_load_w(pr, "wo", wo_d, D)
        for oc in range(NCH):
            for s in range(0, T, 512):
                makes = []
                for k in range(NCH):
                    makes.append(lambda e, k=k, oc=oc, s=s: e.matmul(
                        ps[:, 0:512],
                        wo[:, k * D + oc * 128: k * D + (oc + 1) * 128],
                        o_fm[:, k * T + s: k * T + s + 512],
                        start=(k == 0), stop=(k == NCH - 1)))
                pr.group("pe", makes, reads=[wo, o_fm], writes=[ps])
                pr.dma("qsp", xres.ap, x.h[oc, :, s:s + 512], reads=[x],
                       writes=[xres])
                pr.emit("dve", lambda v: v.tensor_tensor(
                    out=xres.ap, in0=ps[:, 0:512], in1=xres.ap, op=ALU.add),
                    reads=[ps, xres], writes=[xres])
                pr.dma("qsp", xo.h[oc, :, s:s + 512], xres.ap, reads=[xres],
                       writes=[xo])
    if standalone:
        pr.finish()
        pr.__exit__(None, None, None)
        return pr.nc


def run_f1(x_parts, p):
    nc_pre = get_nc("f1_pre", build_f1_pre)
    nc_del = get_nc("f1_delta", build_f1_delta)
    nc_post = get_nc("f1_post", build_f1_post)
    ident = np.eye(128, dtype=np.float32)

    in_a = []
    cwq = np.ascontiguousarray(p["conv_q"].reshape(NCH, 128, 4)
                               .transpose(1, 0, 2).reshape(128, NCH * 4))
    cwk = np.ascontiguousarray(p["conv_k"].reshape(NCH, 128, 4)
                               .transpose(1, 0, 2).reshape(128, NCH * 4))
    cwv = np.ascontiguousarray(p["conv_v"].reshape(NCH, 128, 4)
                               .transpose(1, 0, 2).reshape(128, NCH * 4))
    cw = np.stack([cwq, cwk, cwv], axis=0)

    def firw(f):
        K = f.shape[-1]
        return np.ascontiguousarray(
            f.reshape(1024, K).reshape(NCH, 128, K).transpose(1, 0, 2)
            .reshape(128, NCH * K).astype(np.float32))

    gw_t = chunk_major(np.ascontiguousarray(
        p["gate_w"].transpose(2, 0, 1).reshape(D, 20)))
    base = {
        "wq_t": chunk_major(np.ascontiguousarray(p["Wq"].T)),
        "wk_t": chunk_major(np.ascontiguousarray(p["Wk"].T)),
        "wv_t": chunk_major(np.ascontiguousarray(p["Wv"].T)),
        "wb_t": chunk_major(np.ascontiguousarray(p["Wb"].T)),
        "cw": cw, "fir_s": firw(p["fir_s"]), "fir_m": firw(p["fir_m"]),
        "fir_l": firw(p["fir_l"]), "gw_t": gw_t,
        "gbias": np.ascontiguousarray(
            p["gate_b"].reshape(1, 20).astype(np.float32)),
        "ltemp": np.ascontiguousarray(
            p["log_temp"].reshape(1, 4).astype(np.float32)),
        "epsp": np.ascontiguousarray(
            p["eps_param"].reshape(1, 4).astype(np.float32)),
        "ident": ident,
    }
    for c in range(NCORES):
        b, h = c // 2, c % 2
        m = dict(base)
        cur = x_parts[c].reshape(D, TPC)
        if h == 0:
            halo = np.zeros((D, 512), np.float32)
        else:
            halo = x_parts[c - 1].reshape(D, TPC)[:, TPC - 512:]
        xe = np.concatenate([halo, cur], axis=1)
        m["x_ext"] = np.ascontiguousarray(xe.reshape(NCH, 128, TEXT))
        in_a.append(m)
    ra = run_bass_kernel_spmd(nc_pre, in_a, core_ids=list(range(NCORES)))

    m3 = np.zeros((3, 128, 128), np.float32)
    ii = np.arange(128)[:, None]
    jj = np.arange(128)[None, :]
    m3[0] = -(jj < ii).astype(np.float32)
    m3[1] = -(jj > ii).astype(np.float32)
    m3[2] = (jj >= ii).astype(np.float32)
    in_b = []
    for c in range(NCORES):
        b, hh = c // 2, c % 2
        def full(name, head):
            a0 = ra.results[2 * b][name]
            a1 = ra.results[2 * b + 1][name]
            cat = np.concatenate([a0, a1], axis=2)
            return cat[:, 2 * head:2 * head + 2].transpose(1, 0, 2)
        qn = np.stack([full("qn", 2 * hh + j) for j in range(2)])
        kn = np.stack([full("kn", 2 * hh + j) for j in range(2)])
        vd = np.stack([full("vd", 2 * hh + j) for j in range(2)])
        bt = np.concatenate([ra.results[2 * b]["beta_tm"],
                             ra.results[2 * b + 1]["beta_tm"]], axis=0)
        beta = np.ascontiguousarray(
            np.stack([bt[:, 2 * hh], bt[:, 2 * hh + 1]]))
        in_b.append({"qn": np.ascontiguousarray(qn),
                     "kn": np.ascontiguousarray(kn),
                     "vd": np.ascontiguousarray(vd),
                     "beta": beta, "m3": m3, "ident": ident})
    rb = run_bass_kernel_spmd(nc_del, in_b, core_ids=list(range(NCORES)))

    in_c = []
    nw = np.ascontiguousarray(
        p["norm_w"].reshape(2, 128).T.astype(np.float32))
    wo_t = chunk_major(np.ascontiguousarray(p["Wo"].T))
    for c in range(NCORES):
        b, hh = c // 2, c % 2
        dl = np.empty((128, NCH, TPC), BF)
        for head in range(4):
            src_core = 2 * b + head // 2
            jj_ = head % 2
            d = rb.results[src_core]["delta"][jj_]
            dl[:, 2 * head:2 * head + 2] = d[:, :, hh * TPC:(hh + 1) * TPC] \
                .transpose(1, 0, 2)
        in_c.append({"x": x_parts[c], "rest": ra.results[c]["rest"],
                     "delta": np.ascontiguousarray(dl),
                     "probs": ra.results[c]["probs"], "normw": nw,
                     "wo_t": wo_t})
    rc = run_bass_kernel_spmd(nc_post, in_c, core_ids=list(range(NCORES)))
    return [rc.results[c]["xo"] for c in range(NCORES)]



def to_fm(x):
    out = []
    for c in range(NCORES):
        b, h = c // 2, c % 2
        sl = x[b, h * LLOCAL:(h + 1) * LLOCAL, :]
        fm = np.ascontiguousarray(sl.T.reshape(NCH, 128, TPC))
        out.append(fm)
    return out


def from_fm(parts):
    x = np.empty((4, 4096, D), dtype=np.float32)
    for c in range(NCORES):
        b, h = c // 2, c % 2
        fm = parts[c].reshape(D, TPC)
        x[b, h * LLOCAL:(h + 1) * LLOCAL, :] = fm.T
    return x


def chunk_major(w):
    return np.ascontiguousarray(w.reshape(w.shape[0] // 128, 128, w.shape[1]))


def rope_perm_f7(hd=64):
    sigma = np.empty(hd, np.int64)
    sign = np.empty(hd, np.float32)
    for i in range(hd // 2):
        sigma[2 * i] = 2 * i + 1
        sign[2 * i] = -1.0
        sigma[2 * i + 1] = 2 * i
        sign[2 * i + 1] = 1.0
    return sigma, sign


def rope_tabs(t0, T, sign, hd=64):
    freqs = 10000.0 ** (-np.arange(0, hd, 2, dtype=np.float64) / hd)
    t = np.arange(t0, t0 + T, dtype=np.float64)
    ang = t[:, None] * freqs[None, :]
    j = np.arange(hd)
    ctab = np.cos(ang)[:, j // 2].T
    stab = (np.sin(ang)[:, j // 2] * sign[None, :]).T
    ctab = np.concatenate([ctab, ctab], 0).astype(np.float32)
    stab = np.concatenate([stab, stab], 0).astype(np.float32)
    return np.ascontiguousarray(ctab), np.ascontiguousarray(stab)


def causal_mask4():
    m = np.zeros((4, 128, 512), np.float32)
    for d in range(4):
        delta = d * 128
        k = np.arange(128)[:, None] + delta
        q = np.arange(512)[None, :]
        m[d] = (k <= q).astype(np.float32)
    return m


def rope_perm_f5(hd=64):
    half = hd // 2
    sigma = np.empty(hd, np.int64)
    sign = np.empty(hd, np.float32)
    for j in range(hd):
        if j < half:
            sigma[j] = 2 * j + 1
            sign[j] = -1.0
        else:
            sigma[j] = 2 * (j - half)
            sign[j] = 1.0
    return sigma, sign


def rope_tabs_f5(t0, T, hd=64):
    _, sign = rope_perm_f5(hd)
    return rope_tabs(t0, T, sign, hd)


def ones_blk_np():
    a = np.zeros((128, NCH * 16), np.float32)
    for c in range(NCH):
        a[0:64, c * 16 + 2 * c] = 1.0
        a[64:128, c * 16 + 2 * c + 1] = 1.0
    return a


_nc_cache = {}

def get_nc(name, builder):
    if name not in _nc_cache:
        _nc_cache[name] = builder()
    return _nc_cache[name]


def run_f5(x_parts, w_qkv, w_out, b_out):
    nc_a = get_nc("f5_qkv", build_f5_qkv)
    nc_b = get_nc("f5_attn", build_f5_attn)
    NH, hd = 16, 64
    wq = w_qkv[:, 0:1024]
    wk = w_qkv[:, 1024:2048]
    wv = w_qkv[:, 2048:3072]
    sigma, _ = rope_perm_f5(hd)
    perm = np.concatenate([h * hd + sigma for h in range(NH)])
    wqp = wq[:, perm]
    wkp = wk[:, perm]
    tabs = [rope_tabs_f5(0, TPC), rope_tabs_f5(TPC, TPC)]
    base = {
        "wq_t": chunk_major(wq), "wk_t": chunk_major(wk),
        "wv_t": chunk_major(wv), "wqp_t": chunk_major(wqp),
        "wkp_t": chunk_major(wkp),
    }
    in_a = []
    for c in range(NCORES):
        h = c % 2
        m = dict(base)
        m["x"] = x_parts[c]
        m["ctab"], m["stab"] = tabs[h]
        in_a.append(m)
    ra = run_bass_kernel_spmd(nc_a, in_a, core_ids=list(range(NCORES)))
    wo_t = chunk_major(w_out)
    bo = np.ascontiguousarray(b_out.reshape(NCH, 128).T)
    ob = ones_blk_np()
    in_b = []
    for c in range(NCORES):
        own, oth = c, c ^ 1
        kf_full = np.concatenate(
            [ra.results[own]["kf"], ra.results[oth]["kf"]], axis=2)
        vt_full = np.concatenate(
            [ra.results[own]["vt"], ra.results[oth]["vt"]], axis=0)
        in_b.append({"x": x_parts[c], "qf": ra.results[c]["qf"],
                     "kf": np.ascontiguousarray(kf_full),
                     "vt": np.ascontiguousarray(vt_full),
                     "wo_t": wo_t, "b_out": bo, "ones_blk_in": ob})
    rb = run_bass_kernel_spmd(nc_b, in_b, core_ids=list(range(NCORES)))
    return [rb.results[c]["xo"] for c in range(NCORES)]


def run_f7(x_parts, Wq, Wk, Wv, Wo):
    nc_a = get_nc("f5_qkv", build_f5_qkv)
    nc_b = get_nc("f7_attn", build_f7_attn)
    NH, hd = 16, 64
    wq = np.ascontiguousarray(Wq.T)
    wk = np.ascontiguousarray(Wk.T)
    wv = np.ascontiguousarray(Wv.T)
    sigma, sign = rope_perm_f7(hd)
    perm = np.concatenate([h * hd + sigma for h in range(NH)])
    tabs = [rope_tabs(0, TPC, sign), rope_tabs(TPC, TPC, sign)]
    base = {
        "wq_t": chunk_major(wq), "wk_t": chunk_major(wk),
        "wv_t": chunk_major(wv), "wqp_t": chunk_major(wq[:, perm]),
        "wkp_t": chunk_major(wk[:, perm]),
    }
    in_a = []
    for c in range(NCORES):
        m = dict(base)
        m["x"] = x_parts[c]
        m["ctab"], m["stab"] = tabs[c % 2]
        in_a.append(m)
    ra = run_bass_kernel_spmd(nc_a, in_a, core_ids=list(range(NCORES)))
    wo_t = chunk_major(np.ascontiguousarray(Wo.T))
    bo = np.zeros((128, NCH), np.float32)
    ob = ones_blk_np()
    m4 = causal_mask4()
    in_b = []
    for c in range(NCORES):
        own, oth = c, c ^ 1
        kf_full = np.concatenate(
            [ra.results[own]["kf"], ra.results[oth]["kf"]], axis=2)
        vt_full = np.concatenate(
            [ra.results[own]["vt"], ra.results[oth]["vt"]], axis=0)
        flag = np.full((128, 1), 0.0 if c % 2 == 1 else -1e9, np.float32)
        in_b.append({"x": x_parts[c], "qf": ra.results[c]["qf"],
                     "kf": np.ascontiguousarray(kf_full),
                     "vt": np.ascontiguousarray(vt_full),
                     "wo_t": wo_t, "b_out": bo, "ones_blk_in": ob,
                     "mask4": m4, "oflag": flag})
    rb = run_bass_kernel_spmd(nc_b, in_b, core_ids=list(range(NCORES)))
    return [rb.results[c]["xo"] for c in range(NCORES)]


def col8(b):
    return np.ascontiguousarray(b.reshape(NCH, 128).T.astype(np.float32))


def mask_tiles_f3(t0):
    tiles = np.ones((F3_NMASK, 128, 512), np.float32)
    p = np.arange(128)[:, None]
    f = np.arange(512)[None, :]
    for qb in range(4):
        q_g = t0 + qb * 512 + f
        for d in range(4):
            k_g = t0 + (4 * qb + d) * 128 + p
            blocked = (q_g % 3 == 0) & (k_g > q_g)
            tiles[qb * 4 + d] = 1.0 - blocked.astype(np.float32)
        far = (q_g % 3 != 0).astype(np.float32) * np.ones((128, 1), np.float32)
        tiles[16 + qb] = far
        tiles[20 + qb] = far if t0 == 0 else 1.0
    return tiles


def run_f3(x_parts, p):
    nc_pre = get_nc("f3_pre", build_f3_pre)
    nc_mid = get_nc("f3_mid", build_f3_mid)
    nc_post = get_nc("f3_post", build_f3_post)
    ob = ones_blk_np()
    masks = [mask_tiles_f3(0), mask_tiles_f3(TPC)]
    z8 = np.zeros((128, NCH), np.float32)

    def gbcat(cols):
        return np.ascontiguousarray(np.concatenate(cols, axis=1))

    in_a = []
    for c in range(NCORES):
        gb = gbcat([col8(p["ln1_g"][0]), col8(p["ln1_b"][0]),
                    col8(p["bq"][0]), col8(p["bk"][0]), z8, z8])
        in_a.append({"x": x_parts[c], "wq_t": chunk_major(p["wq"][0]),
                     "wk_t": chunk_major(p["wk"][0]),
                     "wv_t": chunk_major(p["wv"][0]), "gb": gb,
                     "bv_row": np.ascontiguousarray(
                         p["bv"][0].reshape(1, D).astype(np.float32))})
    ra = run_bass_kernel_spmd(nc_pre, in_a, core_ids=list(range(NCORES)))

    in_b = []
    for c in range(NCORES):
        own, oth = c, c ^ 1
        kf_full = np.concatenate(
            [ra.results[own]["kf"], ra.results[oth]["kf"]], axis=2)
        vt_full = np.concatenate(
            [ra.results[own]["vt"], ra.results[oth]["vt"]], axis=0)
        gb = gbcat([col8(p["bo"][0]), col8(p["ln2_g"][0]), col8(p["ln2_b"][0]),
                    col8(p["b2"][0]), col8(p["ln1_g"][1]), col8(p["ln1_b"][1]),
                    col8(p["bq"][1]), col8(p["bk"][1])])
        in_b.append({"x3": ra.results[c]["x3"], "qf": ra.results[c]["qf"],
                     "kf": np.ascontiguousarray(kf_full),
                     "vt": np.ascontiguousarray(vt_full),
                     "masks": masks[c % 2],
                     "wo_t": chunk_major(p["wo"][0]),
                     "w1_t": chunk_major(p["w1"][0]),
                     "w2_t": chunk_major(p["w2"][0]),
                     "wq_t": chunk_major(p["wq"][1]),
                     "wk_t": chunk_major(p["wk"][1]),
                     "wv_t": chunk_major(p["wv"][1]),
                     "gb": gb,
                     "b1c": np.ascontiguousarray(
                         p["b1"][0].reshape(32, 128).T.astype(np.float32)),
                     "bv_row": np.ascontiguousarray(
                         p["bv"][1].reshape(1, D).astype(np.float32)),
                     "ones_blk_in": ob})
    rb = run_bass_kernel_spmd(nc_mid, in_b, core_ids=list(range(NCORES)))

    in_c = []
    for c in range(NCORES):
        own, oth = c, c ^ 1
        kf_full = np.concatenate(
            [rb.results[own]["kf2"], rb.results[oth]["kf2"]], axis=2)
        vt_full = np.concatenate(
            [rb.results[own]["vt2"], rb.results[oth]["vt2"]], axis=0)
        gb = gbcat([col8(p["bo"][1]), col8(p["ln2_g"][1]), col8(p["ln2_b"][1]),
                    col8(p["b2"][1]), col8(p["lnf_g"]), col8(p["lnf_b"])])
        in_c.append({"x": x_parts[c], "x3": rb.results[c]["x3o"],
                     "qf": rb.results[c]["qf2"],
                     "kf": np.ascontiguousarray(kf_full),
                     "vt": np.ascontiguousarray(vt_full),
                     "masks": masks[c % 2],
                     "wo_t": chunk_major(p["wo"][1]),
                     "w1_t": chunk_major(p["w1"][1]),
                     "w2_t": chunk_major(p["w2"][1]),
                     "gb": gb,
                     "b1c": np.ascontiguousarray(
                         p["b1"][1].reshape(32, 128).T.astype(np.float32)),
                     "ones_blk_in": ob})
    rc = run_bass_kernel_spmd(nc_post, in_c, core_ids=list(range(NCORES)))
    return [rc.results[c]["xo"] for c in range(NCORES)]


_f2_nc = None

def run_f2(x_parts, w_gate, w_up, w_down):
    global _f2_nc
    if _f2_nc is None:
        _f2_nc = build_f2()
    wg_t = np.ascontiguousarray(w_gate.T.reshape(NCH, 128, 3072))
    wu_t = np.ascontiguousarray(w_up.T.reshape(NCH, 128, 3072))
    wd_t = np.ascontiguousarray(w_down.T.reshape(24, 128, 1024))
    in_maps = [
        {"x": x_parts[c], "wg_t": wg_t, "wu_t": wu_t, "wd_t": wd_t}
        for c in range(NCORES)
    ]
    res = run_bass_kernel_spmd(_f2_nc, in_maps, core_ids=list(range(NCORES)))
    return [res.results[c]["xo"] for c in range(NCORES)]



PAIRS = [[0, 1], [2, 3], [4, 5], [6, 7]]
ALL8 = [[0, 1, 2, 3, 4, 5, 6, 7]]


def emit_cc(pr, kind, op, groups, in_t, out_t, in_ap=None, out_ap=None):
    pr.emit("pool", lambda e: e.collective_compute(
        kind, op, groups,
        ins=[in_ap if in_ap is not None else in_t.ap],
        outs=[out_ap if out_ap is not None else out_t.ap]),
        reads=[in_t], writes=[out_t], dma=False)


def build_fused():
    T = TPC
    L = 2 * TPC
    pr = Prog("mega")
    with pr:
        tin = {}

        def din(name, shape, dtype=F32):
            tin[name] = pr.dram_in(name, shape, dtype)
            return tin[name]

        x_ext = din("x_ext", [NCH, 128, TEXT])
        for nm in ("f1_wq_t", "f1_wk_t", "f1_wv_t"):
            din(nm, [NCH, 128, D])
        din("f1_wb_t", [NCH, 128, 4])
        din("f1_cw", [3, 128, NCH * 4])
        din("f1_fir_s", [128, NCH * 3])
        din("f1_fir_m", [128, NCH * 15])
        din("f1_fir_l", [128, NCH * 63])
        din("f1_gw_t", [NCH, 128, 20])
        din("f1_gbias", [1, 20])
        din("f1_ltemp", [1, 4])
        din("f1_epsp", [1, 4])
        din("ident", [128, 128])
        din("f1_m3", [3, 128, 128])
        din("f1_normw", [128, 2])
        din("f1_wo_t", [NCH, 128, D])
        din("f2_wg_t", [NCH, 128, 3072], I32)
        din("f2_wu_t", [NCH, 128, 3072], I32)
        din("f2_wd_t", [24, 128, D], I32)
        for i in range(2):
            din(f"f3_wq{i}_t", [NCH, 128, D])
            din(f"f3_wk{i}_t", [NCH, 128, D])
            din(f"f3_wv{i}_t", [NCH, 128, D])
            din(f"f3_wo{i}_t", [NCH, 128, D])
            din(f"f3_w1{i}_t", [NCH, 128, 4096])
            din(f"f3_w2{i}_t", [32, 128, D])
            din(f"f3_b1{i}", [128, 32])
            din(f"f3_bv{i}", [1, D])
        din("f3_gb_pre", [128, 6 * NCH])
        din("f3_gb_mid", [128, 8 * NCH])
        din("f3_gb_post", [128, 6 * NCH])
        din("f3_masks", [40, 128, 512])
        din("ones_blk_in", [128, NCH * 16])
        for nm in ("f5_wq_t", "f5_wk_t", "f5_wv_t", "f5_wqp_t", "f5_wkp_t",
                   "f5_wo_t"):
            din(nm, [NCH, 128, D])
        din("f5_ctab", [128, T])
        din("f5_stab", [128, T])
        din("f5_bo", [128, NCH])
        din("f6_wq_t", [NCH, 128, 2048])
        din("f6_bqc", [128, 16])
        din("f6_gbc", [128, 4])
        din("f6_k0", [32, 128])
        din("f6_k1", [32, 128])
        din("f6_down_t", [NCH, 128, 1024])
        din("f6_up_t", [NCH, 128, 1024])
        for nm in ("f7_wq_t", "f7_wk_t", "f7_wv_t", "f7_wqp_t", "f7_wkp_t",
                   "f7_wo_t"):
            din(nm, [NCH, 128, D])
        din("f7_ctab", [128, T])
        din("f7_stab", [128, T])
        din("f7_oflag", [128, 1])
        din("selfl", [128, 2])
        din("f7_mask8", [8, 128, 512])
        xo_final = pr.dram_out("xo", [NCH, 128, T], F32)

        xs = [pr.dram_tmp(f"xs{i}", [NCH, 128, T], F32) for i in range(8)]
        C = Consts(pr)
        ident_sb = pr.sbuf("ident_g", [128, 128], BF16)
        pr.dma("qpool", ident_sb.ap, tin["ident"].h, reads=[tin["ident"]],
               writes=[ident_sb])
        C.ones_blk = pr.sbuf("ones_blk_g", [128, NCH * 16], BF16)
        pr.dma("qpool", C.ones_blk.ap, tin["ones_blk_in"].h,
               reads=[tin["ones_blk_in"]], writes=[C.ones_blk])

        from concourse.replica_groups import maybe_share_collective_output_space

        def cc_out_space(kind, groups):
            try:
                return maybe_share_collective_output_space(
                    kind, groups, prefer_local=True)
            except Exception:
                return "Local"

        ag_sp = cc_out_space("AllGather", PAIRS)
        cc1_in = pr.dram_tmp("cc1_in", [3072, T], BF16)
        cc1_out = pr.dram_tmp("cc1_out", [6144, T], BF16, addr_space=ag_sp)
        cc2_out = pr.dram_tmp("cc2_out", [2 * T, 4], BF16, addr_space=ag_sp)
        cc3_in = pr.dram_tmp("cc3_in", [512, 2 * T], BF16)
        cc3_out = pr.dram_tmp("cc3_out", [1024, 2 * T], BF16,
                              addr_space=ag_sp)
        qn_d = pr.dram_tmp("f1_qn", [128, NCH, T], BF16)
        kn_d = pr.dram_tmp("f1_kn", [128, NCH, T], BF16)
        vd_d = pr.dram_tmp("f1_vd", [128, NCH, T], BF16)
        bt_d = pr.dram_tmp("f1_bt", [T, 4], BF16)
        pf_d = pr.dram_tmp("f1_pf", [24, T], F32)
        rest_d = pr.dram_tmp("f1_rest", [NCH, 128, T], BF16)

        x_ext_own = Tile(x_ext.h[:, :, 512:], "x_own")

        def f1io(extra):
            m = {"x_ext": x_ext, "ident": tin["ident"], "m3": tin["f1_m3"],
                 "normw": tin["f1_normw"], "probs": pf_d, "rest": rest_d}
            for k in ("wq_t", "wk_t", "wv_t", "wb_t", "cw", "fir_s", "fir_m",
                      "fir_l", "gw_t", "gbias", "ltemp", "epsp", "wo_t"):
                if "f1_" + k in tin:
                    m[k] = tin["f1_" + k]
            m.update(extra)
            return m

        f1sh = {
            "vdir": pr.sbuf("f1g_vdir", [128, NCH * TEXT], BF16),
            "rest_s": pr.sbuf("f1g_rest", [128, NCH * T], BF16),
            "pbc": pr.sbuf("f1g_pbc", [128, T], BF16),
            "prow": pr.sbuf("f1g_prow", [1, 512], F32),
            "ftmp": pr.sbuf("f1g_ftmp", [128, T], F32),
            "fw": pr.sbuf("f1g_fw", [128, NCH * 81], F32),
            "ps_r": pr.psum("f1g_ps_r", [128, 512]),
        }
        with pr.stage("s1a"):
            build_f1_pre(pr, C, f1io({"qn": qn_d, "kn": kn_d, "vd": vd_d,
                                      "beta_tm": bt_d, "shared": f1sh}))
        for ti, dd in enumerate((qn_d, kn_d, vd_d)):
            pr.dma("qsp",
                   cc1_in.h[ti * 1024:(ti + 1) * 1024, :]
                   .rearrange("(c p) t -> c p t", p=128),
                   dd.h.rearrange("p c t -> c p t"),
                   reads=[dd], writes=[cc1_in])
        emit_cc(pr, "AllGather", mybir.AluOpType.bypass, PAIRS, cc1_in,
                cc1_out)
        emit_cc(pr, "AllGather", mybir.AluOpType.bypass, PAIRS, bt_d, cc2_out)
        f1aux = fir_gen(pr, C, f1sh, pf_d, rest_d)
        with pr.stage("s1b"):
            build_f1_delta(pr, C, f1io({"cc1_out": cc1_out,
                                        "cc2_out": cc2_out,
                                        "cc3_in": cc3_in,
                                        "selfl": tin["selfl"]}), cc_mode=True,
                           aux=f1aux)
        for t in ("ps_r", "fw", "ftmp", "prow", "pbc", "rest_s", "vdir"):
            pr.free(f1sh[t])
        emit_cc(pr, "AllGather", mybir.AluOpType.bypass, PAIRS, cc3_in,
                cc3_out)
        with pr.stage("s1c"):
            build_f1_post(pr, C, f1io({"x": x_ext_own, "cc3_out": cc3_out,
                                       "selfl": tin["selfl"],
                                       "xo": xs[0]}), cc_mode=True)

        with pr.stage("s2"):
            build_f2(pr, C, {"x": xs[0], "wg_t": tin["f2_wg_t"],
                             "wu_t": tin["f2_wu_t"], "wd_t": tin["f2_wd_t"],
                             "xo": xs[1]})

        x3_d = pr.dram_tmp("f3_x3", [NCH, 128, T], F32)
        x3o_d = pr.dram_tmp("f3_x3o", [NCH, 128, T], F32)
        f3q = [pr.dram_tmp(f"f3_qf{i}", [128, NCH, T], BF16) for i in range(2)]
        f3k = [pr.dram_tmp(f"f3_kf{i}", [128, NCH, T], BF16) for i in range(2)]
        f3v = [pr.dram_tmp(f"f3_vt{i}", [T // 128, 128, 1040], BF16)
               for i in range(2)]
        ag_sp = cc_out_space("AllGather", PAIRS)
        f3kg = [pr.dram_tmp(f"f3_kg{i}", [2, 128, NCH, T], BF16,
                            addr_space=ag_sp) for i in range(2)]
        f3vg = [pr.dram_tmp(f"f3_vg{i}", [2, T // 128, 128, 1040], BF16,
                            addr_space=ag_sp) for i in range(2)]

        with pr.stage("s3p"):
            build_f3_pre(pr, C, {
                "x": xs[1], "wq_t": tin["f3_wq0_t"], "wk_t": tin["f3_wk0_t"],
                "wv_t": tin["f3_wv0_t"], "gb": tin["f3_gb_pre"],
                "bv_row": tin["f3_bv0"], "x3": x3_d, "qf": f3q[0],
                "kf": f3k[0], "vt": f3v[0]})
        emit_cc(pr, "AllGather", mybir.AluOpType.bypass, PAIRS, f3k[0], f3kg[0])
        emit_cc(pr, "AllGather", mybir.AluOpType.bypass, PAIRS, f3v[0], f3vg[0])
        with pr.stage("s3m"):
            build_f3_mid(pr, C, {
                "x3": x3_d, "qf": f3q[0], "kf_g": f3kg[0], "vt_g": f3vg[0],
                "masks": tin["f3_masks"], "wo_t": tin["f3_wo0_t"],
                "w1_t": tin["f3_w10_t"], "w2_t": tin["f3_w20_t"],
                "wq_t": tin["f3_wq1_t"], "wk_t": tin["f3_wk1_t"],
                "wv_t": tin["f3_wv1_t"], "gb": tin["f3_gb_mid"],
                "b1c": tin["f3_b10"], "bv_row": tin["f3_bv1"],
                "x3o": x3o_d, "qf2": f3q[1], "kf2": f3k[1], "vt2": f3v[1],
                "ones_blk_in": tin["ones_blk_in"]},
                gathered=True, gmask=True)
        emit_cc(pr, "AllGather", mybir.AluOpType.bypass, PAIRS, f3k[1], f3kg[1])
        emit_cc(pr, "AllGather", mybir.AluOpType.bypass, PAIRS, f3v[1], f3vg[1])
        with pr.stage("s3q"):
            build_f3_post(pr, C, {
                "x": xs[1], "x3": x3o_d, "qf": f3q[1], "kf_g": f3kg[1],
                "vt_g": f3vg[1], "masks": tin["f3_masks"],
                "wo_t": tin["f3_wo1_t"], "w1_t": tin["f3_w11_t"],
                "w2_t": tin["f3_w21_t"], "gb": tin["f3_gb_post"],
                "b1c": tin["f3_b11"], "xo": xs[2],
                "ones_blk_in": tin["ones_blk_in"]},
                gathered=True, gmask=True)

        with pr.stage("s4"):
            build_f2(pr, C, {"x": xs[2], "wg_t": tin["f2_wg_t"],
                             "wu_t": tin["f2_wu_t"], "wd_t": tin["f2_wd_t"],
                             "xo": xs[3]})

        f5q_d = pr.dram_tmp("f5_qf", [128, NCH, T], BF16)
        f5k_d = pr.dram_tmp("f5_kf", [128, NCH, T], BF16)
        f5v_d = pr.dram_tmp("f5_vt", [T // 128, 128, 1040], BF16)
        f5kg = pr.dram_tmp("f5_kg", [2, 128, NCH, T], BF16, addr_space=ag_sp)
        f5vg = pr.dram_tmp("f5_vg", [2, T // 128, 128, 1040], BF16,
                           addr_space=ag_sp)
        with pr.stage("s5q"):
            build_f5_qkv(pr, C, {
                "x": xs[3], "wq_t": tin["f5_wq_t"], "wk_t": tin["f5_wk_t"],
                "wv_t": tin["f5_wv_t"], "wqp_t": tin["f5_wqp_t"],
                "wkp_t": tin["f5_wkp_t"], "ctab": tin["f5_ctab"],
                "stab": tin["f5_stab"], "qf": f5q_d, "kf": f5k_d,
                "vt": f5v_d})
        emit_cc(pr, "AllGather", mybir.AluOpType.bypass, PAIRS, f5k_d, f5kg)
        emit_cc(pr, "AllGather", mybir.AluOpType.bypass, PAIRS, f5v_d, f5vg)
        with pr.stage("s5a"):
            bo5 = pr.sbuf("bo5", [128, NCH], F32)
            pr.dma("qsp", bo5.ap, tin["f5_bo"].h, reads=[tin["f5_bo"]],
                   writes=[bo5])
            emit_attn_sublayer(pr, C, "at5", f5q_d, f5kg, f5vg,
                               tin["f5_wo_t"], bo5[:, 0:NCH], xs[3], xs[4],
                               {}, bo_dep=bo5, gathered=True)

        f6_xln = pr.dram_tmp("f6a_xln", [128, NCH, T], BF16)
        f6_q = pr.dram_tmp("f6a_q", [128, 16, T], BF16)
        f6_st = pr.dram_tmp("f6a_st", [128, 4], F32)
        ar_sp = cc_out_space("AllReduce", ALL8)
        f6_sts = pr.dram_tmp("f6a_sts", [128, 4], F32, addr_space=ar_sp)
        with pr.stage("s6q"):
            build_f6_q(pr, C, {"x": xs[4], "wq_t": tin["f6_wq_t"],
                               "bqc": tin["f6_bqc"], "q": f6_q,
                               "xln": f6_xln, "stats": f6_st})
        emit_cc(pr, "AllReduce", mybir.AluOpType.add, ALL8, f6_st, f6_sts)
        with pr.stage("s6m"):
            build_f6_main(pr, C, {
                "x": xs[4], "xln": f6_xln, "q": f6_q, "stsum": f6_sts,
                "gbc": tin["f6_gbc"], "k0": tin["f6_k0"], "k1": tin["f6_k1"],
                "down_t": tin["f6_down_t"], "up_t": tin["f6_up_t"],
                "ident": tin["ident"], "xo": xs[5]}, stats_reduced=True)

        f7q_d = pr.dram_tmp("f7_qf", [128, NCH, T], BF16)
        f7k_d = pr.dram_tmp("f7_kf", [128, NCH, T], BF16)
        f7v_d = pr.dram_tmp("f7_vt", [T // 128, 128, 1040], BF16)
        f7kg = pr.dram_tmp("f7_kg", [2, 128, NCH, T], BF16, addr_space=ag_sp)
        f7vg = pr.dram_tmp("f7_vg", [2, T // 128, 128, 1040], BF16,
                           addr_space=ag_sp)
        with pr.stage("s7q"):
            build_f5_qkv(pr, C, {
                "x": xs[5], "wq_t": tin["f7_wq_t"], "wk_t": tin["f7_wk_t"],
                "wv_t": tin["f7_wv_t"], "wqp_t": tin["f7_wqp_t"],
                "wkp_t": tin["f7_wkp_t"], "ctab": tin["f7_ctab"],
                "stab": tin["f7_stab"], "qf": f7q_d, "kf": f7k_d,
                "vt": f7v_d})
        emit_cc(pr, "AllGather", mybir.AluOpType.bypass, PAIRS, f7k_d, f7kg)
        emit_cc(pr, "AllGather", mybir.AluOpType.bypass, PAIRS, f7v_d, f7vg)
        with pr.stage("s7a"):
            bo7 = pr.sbuf("bo7", [128, NCH], F32)
            pr.emit("dve", lambda v: v.memset(bo7.ap, 0.0), writes=[bo7])
            m8 = [pr.sbuf(f"m8_{i}", [128, 512], BF16) for i in range(8)]
            for i in range(8):
                pr.dma("qpool", m8[i].ap, tin["f7_mask8"].h[i],
                       reads=[tin["f7_mask8"]], writes=[m8[i]])
            of7 = pr.sbuf("of7", [128, 1], F32)
            pr.dma("qsp", of7.ap, tin["f7_oflag"].h, reads=[tin["f7_oflag"]],
                   writes=[of7])

            def f7_region(qb, kt):
                if kt >= 20 + 4 * qb:
                    return "skip"
                if kt < 4 * qb + 4:
                    return "own"
                return "other"

            def f7_mask(qb, kt):
                d = kt - 4 * qb
                if 0 <= d < 4:
                    return m8[d]
                d2 = kt - 16 - 4 * qb
                if 0 <= d2 < 4:
                    return m8[4 + d2]
                return None

            emit_attn_sublayer(pr, C, "at7", f7q_d, f7kg, f7vg,
                               tin["f7_wo_t"], bo7[:, 0:NCH], xs[5], xs[6],
                               dict(region=f7_region, mask=f7_mask,
                                    oflag=of7), bo_dep=bo7, gathered=True)

        f8_xln = pr.dram_tmp("f8a_xln", [128, NCH, T], BF16)
        f8_q = pr.dram_tmp("f8a_q", [128, 16, T], BF16)
        f8_st = pr.dram_tmp("f8a_st", [128, 4], F32)
        f8_sts = pr.dram_tmp("f8a_sts", [128, 4], F32, addr_space=ar_sp)
        with pr.stage("s8q"):
            build_f6_q(pr, C, {"x": xs[6], "wq_t": tin["f6_wq_t"],
                               "bqc": tin["f6_bqc"], "q": f8_q,
                               "xln": f8_xln, "stats": f8_st})
        emit_cc(pr, "AllReduce", mybir.AluOpType.add, ALL8, f8_st, f8_sts)
        with pr.stage("s8m"):
            build_f6_main(pr, C, {
                "x": xs[6], "xln": f8_xln, "q": f8_q, "stsum": f8_sts,
                "gbc": tin["f6_gbc"], "k0": tin["f6_k0"], "k1": tin["f6_k1"],
                "down_t": tin["f6_down_t"], "up_t": tin["f6_up_t"],
                "ident": tin["ident"], "xo": xo_final}, stats_reduced=True)

        pr.finish()
    return pr.nc


def mask_tiles_f3_g(t0):
    tiles = np.ones((40, 128, 512), np.float32)
    p = np.arange(128)[:, None]
    f = np.arange(512)[None, :]

    def allowed(q_g, k_g):
        return 1.0 - ((q_g % 3 == 0) & (k_g > q_g)).astype(np.float32)

    for qb in range(4):
        q_g = t0 + qb * 512 + f
        base = qb * 10
        for d in range(4):
            k_g = (4 * qb + d) * 128 + p
            tiles[base + d] = allowed(q_g, k_g)
        k_rep = (4 * qb + 4) * 128 + np.zeros_like(p)
        tiles[base + 4] = allowed(q_g, k_rep)
        for d in range(4):
            k_g = (16 + 4 * qb + d) * 128 + p
            tiles[base + 5 + d] = allowed(q_g, k_g)
        k_rep = (16 + 4 * qb + 4) * 128 + np.zeros_like(p)
        tiles[base + 9] = allowed(q_g, k_rep)
    return tiles


def mask_tiles_f3_g2(t0):
    tiles = np.ones((40, 128, 512), np.float32)
    p = np.arange(128)[:, None]
    f = np.arange(512)[None, :]

    def allowed(q_g, k_g):
        return 1.0 - ((q_g % 3 == 0) & (k_g > q_g)).astype(np.float32)

    for qb in range(4):
        q_g = t0 + qb * 512 + f
        for j, kb in enumerate((4 * qb, 16 + 4 * qb)):
            for d in range(4):
                k_g = (kb + d) * 128 + p
                tiles[qb * 8 + 4 * j + d] = allowed(q_g, k_g)
        farpat = np.broadcast_to((q_g % 3 != 0).astype(np.float32),
                                 (128, 512)).copy()
        tiles[36 + qb] = farpat
        if t0 == 0:
            tiles[32 + qb] = farpat
    return tiles


def mask8_f7(hh):
    m = np.ones((8, 128, 512), np.float32)
    p = np.arange(128)[:, None]
    f = np.arange(512)[None, :]
    for d in range(4):
        diag = (d * 128 + p <= f).astype(np.float32)
        if hh == 0:
            m[d] = diag
        else:
            m[4 + d] = diag
    return m


_fused_nc = None


def kernel(**inputs):
    global _fused_nc
    I = inputs
    x = np.asarray(I["L0_attn_a_hidden_states"], dtype=np.float32)
    parts = to_fm(x)
    if _fused_nc is None:
        _fused_nc = build_fused()

    NH, hd = 16, 64
    ident = np.eye(128, dtype=np.float32)
    ob = ones_blk_np()
    m3 = np.zeros((3, 128, 128), np.float32)
    ii = np.arange(128)[:, None]
    jj = np.arange(128)[None, :]
    m3[0] = -(jj < ii).astype(np.float32)
    m3[1] = -(jj > ii).astype(np.float32)
    m3[2] = (jj >= ii).astype(np.float32)

    p1 = {k: np.asarray(I["L0_attn_a_" + k]) for k in
          ("Wq", "Wk", "Wv", "Wb", "conv_q", "conv_k", "conv_v", "fir_s",
           "fir_m", "fir_l", "gate_w", "gate_b", "log_temp", "eps_param",
           "norm_w", "Wo")}
    cw = np.stack([np.ascontiguousarray(
        p1[f"conv_{t}"].reshape(NCH, 128, 4).transpose(1, 0, 2)
        .reshape(128, NCH * 4)) for t in ("q", "k", "v")], axis=0)

    def firw(ffl):
        Kk = ffl.shape[-1]
        return np.ascontiguousarray(
            ffl.reshape(1024, Kk).reshape(NCH, 128, Kk).transpose(1, 0, 2)
            .reshape(128, NCH * Kk).astype(np.float32))

    base = {
        "f1_wq_t": chunk_major(np.ascontiguousarray(p1["Wq"].T)),
        "f1_wk_t": chunk_major(np.ascontiguousarray(p1["Wk"].T)),
        "f1_wv_t": chunk_major(np.ascontiguousarray(p1["Wv"].T)),
        "f1_wb_t": chunk_major(np.ascontiguousarray(p1["Wb"].T)),
        "f1_cw": cw, "f1_fir_s": firw(p1["fir_s"]),
        "f1_fir_m": firw(p1["fir_m"]), "f1_fir_l": firw(p1["fir_l"]),
        "f1_gw_t": chunk_major(np.ascontiguousarray(
            p1["gate_w"].transpose(2, 0, 1).reshape(D, 20))),
        "f1_gbias": np.ascontiguousarray(
            p1["gate_b"].reshape(1, 20).astype(np.float32)),
        "f1_ltemp": np.ascontiguousarray(
            p1["log_temp"].reshape(1, 4).astype(np.float32)),
        "f1_epsp": np.ascontiguousarray(
            p1["eps_param"].reshape(1, 4).astype(np.float32)),
        "ident": ident, "f1_m3": m3,
        "f1_normw": np.ascontiguousarray(
            p1["norm_w"].reshape(2, 128).T.astype(np.float32)),
        "f1_wo_t": chunk_major(np.ascontiguousarray(p1["Wo"].T)),
        "ones_blk_in": ob,
    }
    base["f2_wg_t"] = np.ascontiguousarray(
        np.asarray(I["L0_ffn_w_gate"]).T.reshape(NCH, 128, 3072))
    base["f2_wu_t"] = np.ascontiguousarray(
        np.asarray(I["L0_ffn_w_up"]).T.reshape(NCH, 128, 3072))
    base["f2_wd_t"] = np.ascontiguousarray(
        np.asarray(I["L0_ffn_w_down"]).T.reshape(24, 128, 1024))
    p3 = {k: np.asarray(I["L0_attn_b_" + k]) for k in
          ("ln1_g", "ln1_b", "wq", "bq", "wk", "bk", "wv", "bv", "wo", "bo",
           "ln2_g", "ln2_b", "w1", "b1", "w2", "b2", "lnf_g", "lnf_b")}
    z8 = np.zeros((128, NCH), np.float32)

    def gbcat(cols):
        return np.ascontiguousarray(np.concatenate(cols, axis=1))

    for i in range(2):
        base[f"f3_wq{i}_t"] = chunk_major(p3["wq"][i])
        base[f"f3_wk{i}_t"] = chunk_major(p3["wk"][i])
        base[f"f3_wv{i}_t"] = chunk_major(p3["wv"][i])
        base[f"f3_wo{i}_t"] = chunk_major(p3["wo"][i])
        base[f"f3_w1{i}_t"] = chunk_major(p3["w1"][i])
        base[f"f3_w2{i}_t"] = chunk_major(p3["w2"][i])
        base[f"f3_b1{i}"] = np.ascontiguousarray(
            p3["b1"][i].reshape(32, 128).T.astype(np.float32))
        base[f"f3_bv{i}"] = np.ascontiguousarray(
            p3["bv"][i].reshape(1, D).astype(np.float32))
    base["f3_gb_pre"] = gbcat([col8(p3["ln1_g"][0]), col8(p3["ln1_b"][0]),
                               col8(p3["bq"][0]), col8(p3["bk"][0]), z8, z8])
    base["f3_gb_mid"] = gbcat([col8(p3["bo"][0]), col8(p3["ln2_g"][0]),
                               col8(p3["ln2_b"][0]), col8(p3["b2"][0]),
                               col8(p3["ln1_g"][1]), col8(p3["ln1_b"][1]),
                               col8(p3["bq"][1]), col8(p3["bk"][1])])
    base["f3_gb_post"] = gbcat([col8(p3["bo"][1]), col8(p3["ln2_g"][1]),
                                col8(p3["ln2_b"][1]), col8(p3["b2"][1]),
                                col8(p3["lnf_g"]), col8(p3["lnf_b"])])
    w_qkv = np.asarray(I["L1_attn_a_w_qkv"])
    wq5, wk5, wv5 = w_qkv[:, 0:1024], w_qkv[:, 1024:2048], w_qkv[:, 2048:3072]
    sigma5, sign5 = rope_perm_f5(hd)
    perm5 = np.concatenate([h * hd + sigma5 for h in range(NH)])
    base["f5_wq_t"] = chunk_major(wq5)
    base["f5_wk_t"] = chunk_major(wk5)
    base["f5_wv_t"] = chunk_major(wv5)
    base["f5_wqp_t"] = chunk_major(wq5[:, perm5])
    base["f5_wkp_t"] = chunk_major(wk5[:, perm5])
    base["f5_wo_t"] = chunk_major(np.asarray(I["L1_attn_a_w_out"]))
    base["f5_bo"] = np.ascontiguousarray(
        np.asarray(I["L1_attn_a_b_out"]).reshape(NCH, 128).T
        .astype(np.float32))
    base["f6_wq_t"] = chunk_major(np.asarray(I["L1_ffn_Wq"]))
    base["f6_bqc"] = np.ascontiguousarray(
        np.asarray(I["L1_ffn_bq"]).reshape(16, 128).T.astype(np.float32))
    bn_g = np.asarray(I["L1_ffn_bn_gamma"])
    bn_b = np.asarray(I["L1_ffn_bn_beta"])
    base["f6_gbc"] = np.ascontiguousarray(np.stack(
        [bn_g[0:128], bn_g[128:256], bn_b[0:128], bn_b[128:256]],
        axis=1).astype(np.float32))
    base["f6_k0"] = np.asarray(I["L1_ffn_sub_key0"]).astype(np.float32)
    base["f6_k1"] = np.asarray(I["L1_ffn_sub_key1"]).astype(np.float32)
    base["f6_down_t"] = chunk_major(np.ascontiguousarray(
        np.asarray(I["L1_ffn_expert_down"]).T))
    base["f6_up_t"] = chunk_major(np.asarray(I["L1_ffn_expert_up"]))
    wq7 = np.ascontiguousarray(np.asarray(I["L1_attn_b_Wq"]).T)
    wk7 = np.ascontiguousarray(np.asarray(I["L1_attn_b_Wk"]).T)
    wv7 = np.ascontiguousarray(np.asarray(I["L1_attn_b_Wv"]).T)
    sigma7, sign7 = rope_perm_f7(hd)
    perm7 = np.concatenate([h * hd + sigma7 for h in range(NH)])
    base["f7_wq_t"] = chunk_major(wq7)
    base["f7_wk_t"] = chunk_major(wk7)
    base["f7_wv_t"] = chunk_major(wv7)
    base["f7_wqp_t"] = chunk_major(wq7[:, perm7])
    base["f7_wkp_t"] = chunk_major(wk7[:, perm7])
    base["f7_wo_t"] = chunk_major(np.ascontiguousarray(
        np.asarray(I["L1_attn_b_Wo"]).T))

    tabs5 = [rope_tabs_f5(0, TPC), rope_tabs_f5(TPC, TPC)]
    tabs7 = [rope_tabs(0, TPC, sign7), rope_tabs(TPC, TPC, sign7)]
    f3masks = [mask_tiles_f3_g2(0), mask_tiles_f3_g2(TPC)]
    f7m8 = [mask8_f7(0), mask8_f7(1)]

    in_maps = []
    for c in range(NCORES):
        hh = c % 2
        m = dict(base)
        cur = parts[c].reshape(D, TPC)
        halo = (np.zeros((D, 512), np.float32) if hh == 0
                else parts[c - 1].reshape(D, TPC)[:, TPC - 512:])
        m["x_ext"] = np.ascontiguousarray(
            np.concatenate([halo, cur], axis=1).reshape(NCH, 128, TEXT))
        m["f5_ctab"], m["f5_stab"] = tabs5[hh]
        m["f7_ctab"], m["f7_stab"] = tabs7[hh]
        m["f3_masks"] = f3masks[hh]
        m["f7_mask8"] = f7m8[hh]
        m["f7_oflag"] = np.full((128, 1), 0.0 if hh == 1 else -1e9,
                                np.float32)
        sf = np.zeros((128, 2), np.float32)
        sf[:, hh] = 1.0
        m["selfl"] = sf
        in_maps.append(m)
    res = run_bass_kernel_spmd(_fused_nc, in_maps, core_ids=list(range(NCORES)))
    return from_fm([res.results[c]["xo"] for c in range(NCORES)])


def kernel_unfused(**inputs):
    I = inputs
    x = np.asarray(I["L0_attn_a_hidden_states"], dtype=np.float32)
    parts = to_fm(x)

    p1 = {k: np.asarray(I["L0_attn_a_" + k]) for k in
          ("Wq", "Wk", "Wv", "Wb", "conv_q", "conv_k", "conv_v", "fir_s",
           "fir_m", "fir_l", "gate_w", "gate_b", "log_temp", "eps_param",
           "norm_w", "Wo")}
    parts = run_f1(parts, p1)

    wg = np.asarray(I["L0_ffn_w_gate"])
    wu = np.asarray(I["L0_ffn_w_up"])
    wdn = np.asarray(I["L0_ffn_w_down"])
    parts = run_f2(parts, wg, wu, wdn)

    p3 = {k: np.asarray(I["L0_attn_b_" + k2]) for k, k2 in
          (("ln1_g", "ln1_g"), ("ln1_b", "ln1_b"), ("wq", "wq"),
           ("bq", "bq"), ("wk", "wk"), ("bk", "bk"), ("wv", "wv"),
           ("bv", "bv"), ("wo", "wo"), ("bo", "bo"), ("ln2_g", "ln2_g"),
           ("ln2_b", "ln2_b"), ("w1", "w1"), ("b1", "b1"), ("w2", "w2"),
           ("b2", "b2"), ("lnf_g", "lnf_g"), ("lnf_b", "lnf_b"))}
    parts = run_f3(parts, p3)

    parts = run_f2(parts, wg, wu, wdn)

    parts = run_f5(parts, np.asarray(I["L1_attn_a_w_qkv"]),
                   np.asarray(I["L1_attn_a_w_out"]),
                   np.asarray(I["L1_attn_a_b_out"]))

    f6args = (np.asarray(I["L1_ffn_Wq"]), np.asarray(I["L1_ffn_bq"]),
              np.asarray(I["L1_ffn_bn_gamma"]), np.asarray(I["L1_ffn_bn_beta"]),
              np.asarray(I["L1_ffn_sub_key0"]), np.asarray(I["L1_ffn_sub_key1"]),
              np.asarray(I["L1_ffn_expert_down"]),
              np.asarray(I["L1_ffn_expert_up"]))
    parts = run_f6(parts, *f6args)

    parts = run_f7(parts, np.asarray(I["L1_attn_b_Wq"]),
                   np.asarray(I["L1_attn_b_Wk"]),
                   np.asarray(I["L1_attn_b_Wv"]),
                   np.asarray(I["L1_attn_b_Wo"]))

    parts = run_f6(parts, *f6args)
    return from_fm(parts)
